# version 05
# baseline (speedup 1.0000x reference)
from contextlib import ExitStack

import numpy as np
import ml_dtypes
import concourse.bass as bass
import concourse.mybir as mybir
from concourse.bass_utils import run_bass_kernel_spmd

F32 = mybir.dt.float32
BF16 = mybir.dt.bfloat16
AF = mybir.ActivationFunctionType
ALU = mybir.AluOpType
AX = mybir.AxisListType
bfnp = ml_dtypes.bfloat16

C = 2048
G = C // 128
H = 16
WARM = 256
HALO = 3
E = 8
FME = 1408
FC_M = FME // 128
DF = 8192
FC_F = DF // 128
NT = 512

TM = 2048
TH = TM + HALO
TW = TM + WARM
J0 = WARM - HALO
LN_EPS = 1e-5


def tiles(total, n=NT):
    k = (total + n - 1) // n
    base = total // k
    rem = total - base * k
    out, off = [], 0
    for i in range(k):
        sz = base + (1 if i < rem else 0)
        out.append((off, sz))
        off += sz
    return out


ENGINES = ("sync", "gpsimd", "tensor", "vector", "scalar")
SEMS = ("ld", "gg", "mm", "act", "dve", "cc", "st")
_S = {}


class Prog:
    def __init__(self):
        self.ops = {e: [] for e in ENGINES}
        self.cnt = {s: 0 for s in SEMS}

    def add(self, eng, fn):
        self.ops[eng].append(fn)

    def bump(self, sem, by):
        self.cnt[sem] += by
        return self.cnt[sem]

    def wait(self, eng, sem, val=None):
        v = self.cnt[sem] if val is None else val
        if not v:
            return
        self.add(eng, lambda e, s=sem, v=v: e.wait_ge(_S[s], v))

    def wait_all(self, eng, sems=("ld", "gg", "mm", "act", "dve", "st")):
        for s in sems:
            self.wait(eng, s)

    def barrier(self):
        for e in ENGINES:
            self.wait_all(e, SEMS[:-2] + ("st",))


def dma(p, q, out, in_, sem=None):
    if sem is None:
        sem = "ld" if q == "sync" else "gg"
    v = p.bump(sem, 16)

    def f(e, out=out, in_=in_, sem=sem):
        e.dma_start(out=out, in_=in_).then_inc(_S[sem], 16)

    p.add(q, f)
    return v


def op(p, eng, meth, *args, sem=None, **kw):
    v = p.bump(sem, 1) if sem else None
    for k in ("bias", "scale", "alpha"):
        if k in kw and isinstance(kw[k], float) and kw[k] != 0.0:
            kw[k] = np.float32(kw[k])

    def f(e, meth=meth, args=args, kw=kw, sem=sem):
        inst = getattr(e, meth)(*args, **kw)
        if sem:
            inst.then_inc(_S[sem], 1)

    p.add(eng, f)
    return v


class Ctx:
    def __init__(self, nc, p, T, psum):
        self.nc = nc
        self.p = p
        self.T = T
        self.ps = psum
        self.ps_guard = [None] * len(psum)
        self.ps_next = 0
        self.stack = None
        self.cache = {}
        self.gl = {}

    def alloc(self, name, shape, dt):
        key = name
        if key in self.cache:
            return self.cache[key]
        if not hasattr(self, "_names"):
            self._names = {}
        k = self._names.get(name, 0)
        self._names[name] = k + 1
        uname = name if k == 0 else f"{name}__{k}"
        t = self.stack.enter_context(self.nc.sbuf_tensor(uname, list(shape), dt))
        self.cache[key] = t
        return t

    def bank(self, nbanks_away=0):
        i = self.ps_next
        self.ps_next = (self.ps_next + 1) % len(self.ps)
        g = self.ps_guard[i]
        if g:
            self.p.wait("tensor", g[0], g[1])
            self.ps_guard[i] = None
        return i

    def bank_done(self, i, sem, val):
        self.ps_guard[i] = (sem, val)


class Ring:

    def __init__(self, ctx, name, shape, dt, n=4):
        self.bufs = [ctx.alloc(f"{name}_{i}", shape, dt) for i in range(n)]
        self.guard = [None] * n
        self.i = 0
        self.p = ctx.p

    def take(self, eng):
        i = self.i
        self.i = (self.i + 1) % len(self.bufs)
        if self.guard[i]:
            s, v = self.guard[i]
            self.p.wait(eng, s, v)
            self.guard[i] = None
        return self.bufs[i], i

    def done(self, i, sem, val):
        self.guard[i] = (sem, val)


def tensor_specs():
    sp = {}
    A = sp.__setitem__
    A("x_fm", ([C, TW], F32, "in"))
    A("wmask", ([128, 1], F32, "in"))
    A("selmask", ([128, 8], F32, "in"))
    A("td_fm", ([128, G], F32, "in"))
    A("tf_fm", ([128, G], F32, "in"))
    A("wrvkT", ([C, 3 * C], F32, "in"))
    A("woutT", ([C, C], F32, "in"))
    A("gatewT", ([C, E], F32, "in"))
    A("mgateT", ([E, C, FME], F32, "in"))
    A("mupT", ([E, C, FME], F32, "in"))
    A("mdownT", ([E, FME, C], F32, "in"))
    A("wqT", ([C, C], F32, "in"))
    A("wkT", ([C, C], F32, "in"))
    A("wvT", ([C, C], F32, "in"))
    A("woT", ([C, C], F32, "in"))
    A("convw_fm", ([128, G * 4], F32, "in"))
    A("convb_fm", ([128, G], F32, "in"))
    A("igwT", ([C, C], F32, "in"))
    A("igb_fm", ([128, G], F32, "in"))
    A("ogwT", ([C, C], F32, "in"))
    A("ogb_fm", ([128, G], F32, "in"))
    A("gammawT", ([C, H], F32, "in"))
    A("gammab_bc", ([128, H], F32, "in"))
    A("vng", ([128, 1], F32, "in"))
    A("vnb", ([128, 1], F32, "in"))
    A("mng", ([128, 1], F32, "in"))
    A("mnb", ([128, 1], F32, "in"))
    A("gng_fm", ([128, G], F32, "in"))
    A("gnb_fm", ([128, G], F32, "in"))
    A("fupT", ([C, DF], F32, "in"))
    A("fgateT", ([C, DF], F32, "in"))
    A("fdownT", ([DF, C], F32, "in"))
    A("xn", ([C, TW], BF16, "s"))
    A("sr", ([C, TW], BF16, "s"))
    A("vsc", ([C, TW], F32, "s"))
    A("ek", ([C, TW], F32, "s"))
    A("ekv", ([C, TW], F32, "s"))
    A("y", ([C, TH], BF16, "s"))
    A("attn0", ([C, TH], F32, "s"))
    A("tw", ([E * 128, TH], BF16, "s"))
    A("ffn0", ([C, TH], F32, "s"))
    A("x1", ([C, TH], F32, "s"))
    A("xn1", ([C, TH], BF16, "s"))
    A("xc", ([C, TM], BF16, "s"))
    A("qh", ([C, TM], F32, "s"))
    A("kh", ([C, TM], F32, "s"))
    A("vh", ([C, TM], F32, "s"))
    A("qn", ([C, TM], F32, "s"))
    A("kn", ([C, TM], F32, "s"))
    A("vn", ([C, TM], F32, "s"))
    A("ig", ([C, TM], BF16, "s"))
    A("og", ([C, TM], BF16, "s"))
    A("afull", ([C, TM], F32, "s"))
    A("bfull", ([C, TM], F32, "s"))
    A("mem", ([C, TM], F32, "s"))
    A("oo", ([C, TM], BF16, "s"))
    A("attn1", ([C, TM], F32, "s"))
    A("qscales", ([1, 16], F32, "s"))
    A("qup", ([C, DF], BF16, "s"))
    A("qgate", ([C, DF], BF16, "s"))
    A("qdown", ([DF, C], BF16, "s"))
    A("cc_in", ([1, C], F32, "cc"))
    A("cc_out", ([8, C], F32, "cc"))
    A("out_fm", ([C, TM], F32, "out"))
    return sp


def consts(ctx):
    p, nc = ctx.p, ctx.nc
    oc_ = ctx.gl["ones_c"] = ctx.alloc("g_ones_c", (128, 1), BF16)
    or_ = ctx.gl["ones_r"] = ctx.alloc("g_ones_r", (1, 128), F32)
    id_ = ctx.gl["ident"] = ctx.alloc("g_ident", (128, 128), F32)
    a = nc.inline_tensor(np.ones((128, 1), dtype=bfnp), "c_ones_c")
    b = nc.inline_tensor(np.ones((1, 128), dtype=np.float32), "c_ones_r")
    c = nc.inline_tensor(np.eye(128, dtype=np.float32), "c_ident")
    dma(p, "sync", oc_[:, :], a.ap())
    dma(p, "sync", or_[:, :], b.ap())
    v = dma(p, "sync", id_[:, :], c.ap())
    for e in ("tensor", "vector", "scalar"):
        p.wait(e, "ld", v)


def bcast_row(ctx, row_ap, dst_ap, n):
    p = ctx.p
    p.wait("tensor", "act")
    p.wait("tensor", "dve")
    for (o, sz) in tiles(n):
        b = ctx.bank()
        op(p, "tensor", "matmul", ctx.ps[b][:, :sz], ctx.gl["ones_r"][0:1, :],
           row_ap[0:1, o:o + sz], start=True, stop=True, sem="mm")
        p.wait("scalar", "mm")
        if sz >= 128:
            ev = op(p, "scalar", "copy", dst_ap[:, o:o + sz], ctx.ps[b][:, :sz],
                    sem="act")
        else:
            op(p, "scalar", "copy", dst_ap[:, o:o + sz], ctx.ps[b][:, :sz])
            ev = op(p, "scalar", "copy", dst_ap[:, o:o + sz],
                    dst_ap[:, o:o + sz], sem="act")
        ctx.bank_done(b, "act", ev)


def full_ln(ctx, src_name, Tlen, dst_name):
    p = ctx.p
    src, dst = ctx.T[src_name], ctx.T[dst_name]
    xg = ctx.alloc("ln_xg", (128, NT), F32)
    xb = ctx.alloc("ln_xb", (128, NT), BF16)
    sq = ctx.alloc("ln_sq", (128, NT), BF16)
    t1 = ctx.alloc("ln_t1", (128, NT), F32)
    rows = ctx.alloc("ln_rows", (1, 4 * TW), F32)
    rb = ctx.alloc("ln_rb", (128, TW), F32)
    mrb = ctx.alloc("ln_mrb", (128, TW), F32)
    tl = tiles(Tlen)
    for (o, n) in tl:
        bs = ctx.bank()
        bq = ctx.bank()
        for g in range(G):
            p.wait_all("sync", ("act",))
            lv = dma(p, "sync", xg[:, :n], src[g * 128:(g + 1) * 128, o:o + n])
            p.wait("scalar", "ld", lv)
            p.wait("scalar", "mm")
            op(p, "scalar", "copy", xb[:, :n], xg[:, :n])
            av = op(p, "scalar", "activation", sq[:, :n], xg[:, :n], AF.Square,
                    sem="act")
            p.wait("tensor", "act", av)
            op(p, "tensor", "matmul", ctx.ps[bs][0:1, :n],
               ctx.gl["ones_c"][:, 0:1], xb[:, :n], start=(g == 0),
               stop=(g == G - 1), sem="mm" if g == G - 1 else None)
            op(p, "tensor", "matmul", ctx.ps[bq][0:1, :n],
               ctx.gl["ones_c"][:, 0:1], sq[:, :n], start=(g == 0),
               stop=(g == G - 1), sem="mm" if g == G - 1 else None)
        p.wait("scalar", "mm")
        ev = op(p, "scalar", "copy", rows[0:1, o:o + n], ctx.ps[bs][0:1, :n],
                sem="act")
        e2 = op(p, "scalar", "copy", rows[0:1, TW + o:TW + o + n],
                ctx.ps[bq][0:1, :n], sem="act")
        ctx.bank_done(bs, "act", ev)
        ctx.bank_done(bq, "act", e2)
    sA = rows[0:1, 0:Tlen]
    sQ = rows[0:1, TW:TW + Tlen]
    rR = rows[0:1, 2 * TW:2 * TW + Tlen]
    rM = rows[0:1, 3 * TW:3 * TW + Tlen]
    p.wait("vector", "act")
    op(p, "vector", "tensor_scalar_mul", sA, sA, 1.0 / C)
    op(p, "vector", "tensor_scalar_mul", sQ, sQ, 1.0 / C)
    op(p, "vector", "tensor_mul", rR, sA, sA)
    op(p, "vector", "tensor_sub", sQ, sQ, rR, sem="dve")
    p.wait("scalar", "dve")
    op(p, "scalar", "activation", rR, sQ, AF.Ln, bias=LN_EPS)
    op(p, "scalar", "activation", rR, rR, AF.Exp, scale=-0.5, sem="act")
    p.wait("vector", "act")
    op(p, "vector", "tensor_mul", rM, sA, rR, sem="dve")
    bcast_row(ctx, rR, rb, Tlen)
    bcast_row(ctx, rM, mrb, Tlen)
    for g in range(G):
        for (o, n) in tl:
            p.wait_all("sync", ("dve",))
            lv = dma(p, "sync", xg[:, :n], src[g * 128:(g + 1) * 128, o:o + n])
            p.wait("vector", "ld", lv)
            p.wait("vector", "act")
            p.wait("vector", "st")
            op(p, "vector", "tensor_mul", t1[:, :n], xg[:, :n], rb[:, o:o + n])
            dv = op(p, "vector", "tensor_sub", xb[:, :n], t1[:, :n],
                    mrb[:, o:o + n], sem="dve")
            p.wait("sync", "dve", dv)
            dma(p, "sync", dst[g * 128:(g + 1) * 128, o:o + n], xb[:, :n],
                sem="st")


def head_stats(ctx, xg_ap, n, rows, want_mean, l2=False, eps=LN_EPS):
    p = ctx.p
    xb = ctx.alloc("hs_xb", (128, NT), BF16)
    sq = ctx.alloc("hs_sq", (128, NT), BF16)
    rQ = rows[0:1, 0:n]
    rS = rows[0:1, TM:TM + n]
    r2 = rows[0:1, 2 * TM:2 * TM + n]
    for (o, sz) in tiles(n):
        bq = ctx.bank()
        bs = ctx.bank() if want_mean else None
        p.wait("scalar", "mm")
        av = op(p, "scalar", "activation", sq[:, :sz], xg_ap[:, o:o + sz],
                AF.Square, sem="act")
        if want_mean:
            av = op(p, "scalar", "copy", xb[:, :sz], xg_ap[:, o:o + sz],
                    sem="act")
        p.wait("tensor", "act", av)
        mv = op(p, "tensor", "matmul", ctx.ps[bq][0:1, :sz],
                ctx.gl["ones_c"][:, 0:1], sq[:, :sz], start=True, stop=True,
                sem="mm")
        if want_mean:
            mv = op(p, "tensor", "matmul", ctx.ps[bs][0:1, :sz],
                    ctx.gl["ones_c"][:, 0:1], xb[:, :sz], start=True,
                    stop=True, sem="mm")
        p.wait("vector", "mm", mv)
        dv = op(p, "vector", "tensor_scalar_mul", rQ[0:1, o:o + sz],
                ctx.ps[bq][0:1, :sz], 1.0 if l2 else 1.0 / 128, sem="dve")
        ctx.bank_done(bq, "dve", dv)
        if want_mean:
            dv = op(p, "vector", "tensor_scalar_mul", rS[0:1, o:o + sz],
                    ctx.ps[bs][0:1, :sz], 1.0 / 128, sem="dve")
            ctx.bank_done(bs, "dve", dv)
    if want_mean:
        op(p, "vector", "tensor_mul", r2, rS, rS)
        op(p, "vector", "tensor_sub", rQ, rQ, r2, sem="dve")
    if l2:
        op(p, "vector", "tensor_scalar_max", rQ, rQ, 1e-24, sem="dve")
    p.wait("scalar", "dve")
    op(p, "scalar", "activation", rQ, rQ, AF.Ln,
       bias=0.0 if l2 else eps)
    op(p, "scalar", "activation", rQ, rQ, AF.Exp, scale=-0.5, sem="act")
    return rQ, rS


def stream_wcols(ctx, p, wap, oc, kgroups, wring):
    buf, slot = wring.take("gpsimd")
    v = dma(p, "gpsimd", buf[:, :, :],
            wap[:, oc * 128:(oc + 1) * 128].rearrange("(kc p) m -> p kc m",
                                                      p=128))
    return buf, slot, v


def proj_stream(ctx, wname, src_res_fn, Tlen, evac, ocs=G, kgroups=G, woff=0):
    p = ctx.p
    W = ctx.T[wname]
    wring = Ring(ctx, f"wr_{wname}", (128, kgroups, 128), BF16, n=2)
    tl = tiles(Tlen)
    for oc in range(ocs):
        buf, slot, gv = stream_wcols(ctx, p, W[:, woff:woff + ocs * 128], oc,
                                     kgroups, wring)
        p.wait("tensor", "gg", gv)
        last = None
        for ti, (o, n) in enumerate(tl):
            b = ctx.bank()
            for kc in range(kgroups):
                last = op(p, "tensor", "matmul", ctx.ps[b][:, :n],
                          buf[:, kc, :], src_res_fn(kc, o, n),
                          start=(kc == 0), stop=(kc == kgroups - 1),
                          sem="mm" if kc == kgroups - 1 else None)
            evac(oc, o, n, b)
        wring.done(slot, "mm", last)


PHASES = []


def phase(name, reads, writes):
    def deco(fn):
        PHASES.append((name, reads, writes, fn))
        return fn
    return deco


@phase("ln0", ["x_fm"], ["xn"])
def ph_ln0(ctx):
    full_ln(ctx, "x_fm", TW, "xn")


@phase("qscale", ["fupT", "fgateT", "fdownT"], ["qscales"])
def ph_qscale(ctx):
    p = ctx.p
    wq = ctx.alloc("qs_w", (128, DF), BF16)
    rows = ctx.alloc("qs_rows", (1, 16), F32)
    for wi, wname in enumerate(("fupT", "fgateT", "fdownT")):
        W = ctx.T[wname]
        R, Cc = tensor_specs()[wname][0]
        b = ctx.bank()
        first = True
        for r0 in range(0, R, 128):
            p.wait_all("gpsimd", ("mm",))
            lv = dma(p, "gpsimd", wq[:, :Cc], W[r0:r0 + 128, :])
            p.wait("vector", "gg", lv)
            p.wait("vector", "mm")
            dv = op(p, "vector", "tensor_single_scalar",
                    wq[:, :Cc].bitcast(mybir.dt.uint16),
                    wq[:, :Cc].bitcast(mybir.dt.uint16), 0x7FFF,
                    ALU.bitwise_and, sem="dve")
            p.wait("tensor", "dve", dv)
            tl_q = tiles(Cc)
            for (o, n) in tl_q:
                lastmm = (r0 + 128 >= R) and (o + n >= Cc)
                chunk_last = (o + n >= Cc)
                op(p, "tensor", "matmul", ctx.ps[b][0:1, 0:n],
                   ctx.gl["ones_c"][:, 0:1], wq[:, o:o + n],
                   start=first, stop=lastmm,
                   sem="mm" if chunk_last else None)
                first = False
        p.wait("vector", "mm")
        dv = op(p, "vector", "tensor_reduce", rows[0:1, wi:wi + 1],
                ctx.ps[b][0:1, 0:NT], AX.X, ALU.add, sem="dve")
        p.wait("vector", "dve", dv)
        dv = op(p, "vector", "tensor_scalar", rows[0:1, 4 + wi:5 + wi],
                rows[0:1, wi:wi + 1], 1.0 / (R * Cc), 1e-8, ALU.mult, ALU.add,
                sem="dve")
        p.wait("vector", "dve", dv)
        ctx.bank_done(b, "dve", dv)
    for wi in range(3):
        dv = op(p, "vector", "tensor_scalar_mul", rows[0:1, 8 + wi:9 + wi],
                rows[0:1, 4 + wi:5 + wi], 0.5, sem="dve")
        p.wait("vector", "dve", dv)
    dv = op(p, "vector", "tensor_copy", rows[0:1, 11:12], rows[0:1, 4:5],
            sem="dve")
    p.wait("sync", "dve", dv)
    dma(p, "sync", ctx.T["qscales"][0:1, 0:16], rows[0:1, 0:16], sem="st")


@phase("qquant", ["fupT", "fgateT", "fdownT", "qscales"],
       ["qup", "qgate", "qdown"])
def ph_qquant(ctx):
    p = ctx.p
    rows = ctx.alloc("qq_rows", (1, 16), F32)
    thrn = ctx.alloc("qq_thrn", (1, 4), F32)
    lv = dma(p, "sync", rows[0:1, 0:16], ctx.T["qscales"][0:1, 0:16])
    p.wait("vector", "ld", lv)
    for wi in range(3):
        op(p, "vector", "tensor_scalar_mul", thrn[0:1, wi:wi + 1],
           rows[0:1, 8 + wi:9 + wi], -1.0, sem="dve" if wi == 2 else None)
    thrc = ctx.alloc("qq_thrc", (128, 4), F32)
    thrcn = ctx.alloc("qq_thrcn", (128, 4), F32)
    for wi in range(3):
        bcast_row(ctx, rows[0:1, 8 + wi:9 + wi], thrc[:, wi:wi + 1], 1)
        bcast_row(ctx, thrn[0:1, wi:wi + 1], thrcn[:, wi:wi + 1], 1)
    t1 = ctx.alloc("qq_t1", (128, DF), F32)
    wring = Ring(ctx, "qq_wf", (128, DF), F32, n=2)
    qring = Ring(ctx, "qq_qb", (128, DF), BF16, n=2)
    for wi, (wname, qname) in enumerate(
            (("fupT", "qup"), ("fgateT", "qgate"), ("fdownT", "qdown"))):
        W, Q = ctx.T[wname], ctx.T[qname]
        R, Cc = tensor_specs()[wname][0]
        for r0 in range(0, R, 128):
            wf, ws_ = wring.take("sync")
            lv = dma(p, "sync", wf[:, :Cc], W[r0:r0 + 128, :])
            p.wait("vector", "ld", lv)
            op(p, "vector", "tensor_single_scalar", t1[:, :Cc], wf[:, :Cc],
               thrc[:, wi:wi + 1], ALU.is_ge)
            op(p, "vector", "tensor_single_scalar", wf[:, :Cc], wf[:, :Cc],
               thrcn[:, wi:wi + 1], ALU.is_le)
            qb, qs_ = qring.take("vector")
            dv = op(p, "vector", "tensor_sub", qb[:, :Cc], t1[:, :Cc],
                    wf[:, :Cc], sem="dve")
            wring.done(ws_, "dve", dv)
            p.wait("sync", "dve", dv)
            sv = dma(p, "sync", Q[r0:r0 + 128, :], qb[:, :Cc], sem="st")
            qring.done(qs_, "st", sv)


@phase("rvk", ["xn", "wrvkT"], ["sr", "vsc", "ek", "ekv"])
def ph_rvk(ctx):
    p = ctx.p
    xr = ctx.alloc("rvk_xn", (128, G, TW), BF16)
    lv = dma(p, "sync", xr[:, :, :],
             ctx.T["xn"].rearrange("(g p) t -> p g t", p=128))
    p.wait("tensor", "ld", lv)
    rbf = Ring(ctx, "rvk_eb", (128, NT), BF16, n=3)
    rf1 = Ring(ctx, "rvk_ef", (128, NT), F32, n=3)
    rf2 = Ring(ctx, "rvk_e2", (128, NT), F32, n=3)
    rvr = Ring(ctx, "rvk_vr", (128, NT), F32, n=3)

    def evac(oc, o, n, b):
        gi = oc % G
        r = slice(gi * 128, (gi + 1) * 128)
        p.wait("scalar", "mm")
        if oc < G:
            buf, s = rbf.take("scalar")
            ev = op(p, "scalar", "activation", buf[:, :n], ctx.ps[b][:, :n],
                    AF.Sigmoid, sem="act")
            p.wait("sync", "act", ev)
            sv = dma(p, "sync", ctx.T["sr"][r, o:o + n], buf[:, :n], sem="st")
            rbf.done(s, "st", sv)
            ctx.bank_done(b, "act", ev)
        elif oc < 2 * G:
            buf, s = rf1.take("scalar")
            ev = op(p, "scalar", "copy", buf[:, :n], ctx.ps[b][:, :n], sem="act")
            p.wait("sync", "act", ev)
            sv = dma(p, "sync", ctx.T["vsc"][r, o:o + n], buf[:, :n], sem="st")
            rf1.done(s, "st", sv)
            ctx.bank_done(b, "act", ev)
        else:
            buf, s = rf1.take("scalar")
            ev = op(p, "scalar", "activation", buf[:, :n], ctx.ps[b][:, :n],
                    AF.Exp, sem="act")
            ctx.bank_done(b, "act", ev)
            vbuf, vs = rvr.take("sync")
            lv2 = dma(p, "sync", vbuf[:, :n], ctx.T["vsc"][r, o:o + n])
            p.wait("sync", "act", ev)
            sv = dma(p, "sync", ctx.T["ek"][r, o:o + n], buf[:, :n], sem="st")
            p.wait("vector", "act", ev)
            p.wait("vector", "ld", lv2)
            obuf, os_ = rf2.take("vector")
            dv = op(p, "vector", "tensor_mul", obuf[:, :n], buf[:, :n],
                    vbuf[:, :n], sem="dve")
            rf1.done(s, "dve", dv)
            rvr.done(vs, "dve", dv)
            p.wait("sync", "dve", dv)
            s2 = dma(p, "sync", ctx.T["ekv"][r, o:o + n], obuf[:, :n], sem="st")
            rf2.done(os_, "st", s2)

    proj_stream(ctx, "wrvkT", lambda kc, o, n: xr[:, kc, o:o + n], TW, evac,
                ocs=3 * G)


@phase("wkv", ["ek", "ekv", "sr", "td_fm", "tf_fm", "wmask"], ["y"])
def ph_wkv(ctx):
    p = ctx.p
    tdc = ctx.alloc("wk_td", (128, G), F32)
    tfc = ctx.alloc("wk_tf", (128, G), F32)
    wmc = ctx.alloc("wk_wm", (128, 1), F32)
    alph = ctx.alloc("wk_al", (128, G), F32)
    euc = ctx.alloc("wk_eu", (128, G), F32)
    dma(p, "sync", tdc[:, :], ctx.T["td_fm"])
    dma(p, "sync", tfc[:, :], ctx.T["tf_fm"])
    lv = dma(p, "sync", wmc[:, :], ctx.T["wmask"])
    p.wait("scalar", "ld", lv)
    op(p, "scalar", "activation", alph[:, :], tdc[:, :], AF.Exp)
    op(p, "scalar", "activation", alph[:, :], alph[:, :], AF.Exp, scale=-1.0)
    op(p, "scalar", "activation", euc[:, :], tfc[:, :], AF.Exp)
    op(p, "scalar", "copy", alph[:, :], alph[:, :])
    av = op(p, "scalar", "copy", euc[:, :], euc[:, :], sem="act")
    ekr = Ring(ctx, "wk_ek", (128, TW), F32, n=2)
    ekvr = Ring(ctx, "wk_ekv", (128, TW), F32, n=2)
    srr = Ring(ctx, "wk_sr", (128, TH), BF16, n=2)
    yr_ = Ring(ctx, "wk_y", (128, TH), BF16, n=2)
    sS = ctx.alloc("wk_S", (128, TW), F32)
    sZ = ctx.alloc("wk_Z", (128, TW), F32)
    al = ctx.alloc("wk_ab", (128, TW), F32)
    onesw = ctx.alloc("wk_1", (128, TW), F32)
    num = ctx.alloc("wk_nu", (128, TH), F32)
    den = ctx.alloc("wk_de", (128, TH), F32)
    p.wait("vector", "act", av)
    op(p, "vector", "memset", onesw[:, :], 1.0)
    for g in range(G):
        r = slice(g * 128, (g + 1) * 128)
        ekg, eks = ekr.take("sync")
        ekvg, ekvs = ekvr.take("sync")
        srg, srs = srr.take("sync")
        dma(p, "sync", ekg[:, :], ctx.T["ek"][r, :])
        dma(p, "sync", ekvg[:, :], ctx.T["ekv"][r, :])
        l3 = dma(p, "sync", srg[:, :], ctx.T["sr"][r, J0:TW])
        p.wait("vector", "ld", l3)
        op(p, "vector", "tensor_scalar_mul", al[:, :], onesw[:, :],
           alph[:, g:g + 1])
        op(p, "vector", "tensor_scalar_mul", ekg[:, 0:WARM], ekg[:, 0:WARM],
           wmc[:, 0:1])
        op(p, "vector", "tensor_scalar_mul", ekvg[:, 0:WARM], ekvg[:, 0:WARM],
           wmc[:, 0:1])
        op(p, "vector", "tensor_tensor_scan", sS[:, :], al[:, :], ekvg[:, :],
           0.0, ALU.mult, ALU.add)
        dv = op(p, "vector", "tensor_tensor_scan", sZ[:, :], al[:, :],
                ekg[:, :], 0.0, ALU.mult, ALU.add, sem="dve")
        p.wait("vector", "dve", dv)
        op(p, "vector", "scalar_tensor_tensor", num[:, :], ekvg[:, J0:TW],
           euc[:, g:g + 1], sS[:, J0 - 1:TW - 1], ALU.mult, ALU.add)
        dv = op(p, "vector", "scalar_tensor_tensor", den[:, :], ekg[:, J0:TW],
                euc[:, g:g + 1], sZ[:, J0 - 1:TW - 1], ALU.mult, ALU.add,
                sem="dve")
        ekr.done(eks, "dve", dv)
        ekvr.done(ekvs, "dve", dv)
        dv = op(p, "vector", "tensor_scalar_max", den[:, :], den[:, :], 1e-30,
                sem="dve")
        p.wait("scalar", "dve", dv)
        op(p, "scalar", "activation", den[:, :], den[:, :], AF.Ln)
        av = op(p, "scalar", "activation", den[:, :], den[:, :], AF.Exp,
                scale=-1.0, sem="act")
        p.wait("vector", "act", av)
        op(p, "vector", "tensor_mul", num[:, :], num[:, :], den[:, :])
        ybf, ys = yr_.take("vector")
        dv = op(p, "vector", "tensor_mul", ybf[:, :], num[:, :], srg[:, :],
                sem="dve")
        srr.done(srs, "dve", dv)
        p.wait("sync", "dve", dv)
        sv = dma(p, "sync", ctx.T["y"][r, :], ybf[:, :], sem="st")
        yr_.done(ys, "st", sv)


def make_copy_evac(ctx, dst_name, ring):
    p = ctx.p

    def evac(oc, o, n, b):
        r = slice(oc * 128, (oc + 1) * 128)
        p.wait("scalar", "mm")
        buf, s = ring.take("scalar")
        ev = op(p, "scalar", "copy", buf[:, :n], ctx.ps[b][:, :n], sem="act")
        ctx.bank_done(b, "act", ev)
        p.wait("sync", "act", ev)
        sv = dma(p, "sync", ctx.T[dst_name][r, o:o + n], buf[:, :n], sem="st")
        ring.done(s, "st", sv)
    return evac


@phase("wout", ["y", "woutT"], ["attn0"])
def ph_wout(ctx):
    p = ctx.p
    yr = ctx.alloc("wo_y", (128, G, TH), BF16)
    lv = dma(p, "sync", yr[:, :, :],
             ctx.T["y"].rearrange("(g p) t -> p g t", p=128))
    p.wait("tensor", "ld", lv)
    ring = Ring(ctx, "wo_ev", (128, NT), F32, n=3)
    proj_stream(ctx, "woutT", lambda kc, o, n: yr[:, kc, o:o + n], TH,
                make_copy_evac(ctx, "attn0", ring))


@phase("router", ["xn", "gatewT"], ["tw"])
def ph_router(ctx):
    p = ctx.p
    gw = ctx.alloc("rt_gw", (128, G, E), BF16)
    lv = dma(p, "gpsimd", gw[:, :, :],
             ctx.T["gatewT"].rearrange("(g p) e -> p g e", p=128))
    xt = ctx.alloc("rt_xn", (128, 128), BF16)
    lg = ctx.alloc("rt_lg", (128, E), F32)
    twt = ctx.alloc("rt_tw", (128, E), F32)
    rex = ctx.alloc("rt_ex", (128, E), F32)
    r1 = ctx.alloc("rt_r1", (128, 2), F32)
    twT = ctx.alloc("rt_twT", (E, TH), F32)
    p.wait("tensor", "gg", lv)
    ntile = (TH + 127) // 128
    for ti in range(ntile):
        o = J0 + ti * 128
        n = min(128, TW - o)
        b = ctx.bank()
        for g in range(G):
            p.wait_all("sync", ("mm",))
            l2 = dma(p, "sync", xt[:, :n],
                     ctx.T["xn"][g * 128:(g + 1) * 128, o:o + n])
            p.wait("tensor", "ld", l2)
            op(p, "tensor", "matmul", ctx.ps[b][0:n, 0:E], xt[:, :n],
               gw[:, g, :], start=(g == 0), stop=(g == G - 1), sem="mm")
        p.wait("vector", "mm")
        p.wait("vector", "act")

        def vstep(meth, *a, **kw):
            _d = op(p, "vector", meth, *a, sem="dve", **kw)
            p.wait("vector", "dve", _d)
            return _d

        vstep("tensor_copy", lg[:n, :], ctx.ps[b][0:n, 0:E])
        ctx.bank_done(b, "dve", p.cnt["dve"])
        vstep("reduce_max", r1[:n, 0:1], lg[:n, :], AX.X)
        vstep("tensor_scalar_sub", lg[:n, :], lg[:n, :], r1[:n, 0:1])
        vstep("tensor_single_scalar", twt[:n, :], lg[:n, :], 0.0, ALU.is_ge)
        vstep("tensor_scalar_mul", twt[:n, :], twt[:n, :], -1e30)
        vstep("tensor_add", twt[:n, :], twt[:n, :], lg[:n, :])
        dv = vstep("reduce_max", r1[:n, 1:2], twt[:n, :], AX.X)
        p.wait("scalar", "dve", dv)
        av = op(p, "scalar", "activation", rex[:n, :], lg[:n, :], AF.Exp,
                sem="act")
        p.wait("vector", "act", av)
        vstep("tensor_single_scalar", twt[:n, :], lg[:n, :], r1[:n, 1:2],
              ALU.is_ge)
        vstep("tensor_mul", twt[:n, :], twt[:n, :], rex[:n, :])
        dv = vstep("tensor_reduce", r1[:n, 0:1], twt[:n, :], AX.X, ALU.add)
        p.wait("scalar", "dve", dv)
        p.wait("scalar", "act")
        op(p, "scalar", "activation", r1[:n, 0:1], r1[:n, 0:1], AF.Ln)
        op(p, "scalar", "activation", r1[:n, 0:1], r1[:n, 0:1], AF.Exp,
           scale=-1.0)
        av = op(p, "scalar", "copy", r1[:n, 0:1], r1[:n, 0:1], sem="act")
        p.wait("vector", "act", av)
        dv = vstep("tensor_scalar_mul", twt[:n, :], twt[:n, :], r1[:n, 0:1])
        p.wait("tensor", "dve", dv)
        b2 = ctx.bank()
        op(p, "tensor", "transpose", ctx.ps[b2][0:E, 0:128], twt[:, 0:E],
           ctx.gl["ident"][:, :], sem="mm")
        p.wait("vector", "mm")
        dv = op(p, "vector", "tensor_copy", twT[0:E, ti * 128:ti * 128 + n],
                ctx.ps[b2][0:E, 0:n], sem="dve")
        ctx.bank_done(b2, "dve", dv)
    row1 = ctx.alloc("rt_row", (1, TH), F32)
    ring = Ring(ctx, "rt_ev", (128, NT), BF16, n=3)
    for e in range(E):
        p.wait_all("sync", ("dve", "ld", "mm"))
        l3 = dma(p, "sync", row1[0:1, :], twT[e:e + 1, 0:TH])
        p.wait("tensor", "ld", l3)
        for (o, n) in tiles(TH):
            b = ctx.bank()
            op(p, "tensor", "matmul", ctx.ps[b][:, :n], ctx.gl["ones_r"][0:1, :],
               row1[0:1, o:o + n], start=True, stop=True, sem="mm")
            p.wait("scalar", "mm")
            buf, s = ring.take("scalar")
            ev = op(p, "scalar", "copy", buf[:, :n], ctx.ps[b][:, :n], sem="act")
            ctx.bank_done(b, "act", ev)
            p.wait("sync", "act", ev)
            sv = dma(p, "sync", ctx.T["tw"][e * 128:(e + 1) * 128, o:o + n],
                     buf[:, :n], sem="st")
            ring.done(s, "st", sv)


@phase("moe", ["xn", "tw", "mgateT", "mupT", "mdownT"], ["ffn0"])
def ph_moe(ctx):
    p = ctx.p
    HL = 1027
    acc = ctx.alloc("mo_acc", (128, G, HL), F32)
    hall = ctx.alloc("mo_h", (128, FC_M, HL), BF16)
    twb = ctx.alloc("mo_tw", (128, HL), BF16)
    xnt = ctx.alloc("mo_xn", (128, G, HL), BF16)
    gring = Ring(ctx, "mo_wg", (128, G, 128), BF16, n=2)
    uring = Ring(ctx, "mo_wu", (128, G, 128), BF16, n=2)
    dring = Ring(ctx, "mo_wd", (128, FC_M, 128), BF16, n=2)
    sb1 = ctx.alloc("mo_e1", (128, NT), BF16)
    sb2 = ctx.alloc("mo_e2", (128, NT), BF16)
    for (h0, hn) in ((0, 1024), (1024, TH - 1024)):
        p.wait_all("vector", ("dve", "mm", "st"))
        op(p, "vector", "memset", acc[:, :, 0:hn], 0.0, sem="dve")
        htl = tiles(hn)
        p.wait_all("sync", ("mm",))
        for g in range(G):
            dma(p, "sync", xnt[:, g, 0:hn],
                ctx.T["xn"][g * 128:(g + 1) * 128, J0 + h0:J0 + h0 + hn])
        lvx_all = p.cnt["ld"]
        for e in range(E):
            p.wait_all("gpsimd", ("dve",))
            lvtw = dma(p, "gpsimd", twb[:, 0:hn],
                       ctx.T["tw"][e * 128:(e + 1) * 128, h0:h0 + hn])
            lastdv = None
            for fc in range(FC_M):
                gb, gs, v1 = stream_wcols(
                    ctx, p, ctx.T["mgateT"][e, :, fc * 128:(fc + 1) * 128]
                    if False else ctx.T["mgateT"][e], fc, G, gring)
                ub, us, v2 = stream_wcols(ctx, p, ctx.T["mupT"][e], fc, G,
                                          uring)
                p.wait("tensor", "gg", v2)
                lastg = lastu = None
                for (o, n) in htl:
                    p.wait("tensor", "ld", lvx_all)
                    bg = ctx.bank()
                    bu = ctx.bank()
                    for g in range(G):
                        lastg = op(p, "tensor", "matmul", ctx.ps[bg][:, :n],
                                   gb[:, g, :], xnt[:, g, o:o + n],
                                   start=(g == 0), stop=(g == G - 1),
                                   sem="mm" if g == G - 1 else None)
                    for g in range(G):
                        lastu = op(p, "tensor", "matmul", ctx.ps[bu][:, :n],
                                   ub[:, g, :], xnt[:, g, o:o + n],
                                   start=(g == 0), stop=(g == G - 1),
                                   sem="mm" if g == G - 1 else None)
                    p.wait("scalar", "mm")
                    p.wait("scalar", "dve")
                    op(p, "scalar", "activation", sb1[:, :n], ctx.ps[bg][:, :n],
                       AF.Silu)
                    av = op(p, "scalar", "copy", sb2[:, :n], ctx.ps[bu][:, :n],
                            sem="act")
                    ctx.bank_done(bg, "act", av)
                    ctx.bank_done(bu, "act", av)
                    p.wait("vector", "act", av)
                    p.wait("vector", "gg", lvtw)
                    op(p, "vector", "tensor_mul", sb1[:, :n], sb1[:, :n],
                       sb2[:, :n])
                    lastdv = op(p, "vector", "tensor_mul",
                                hall[:, fc, o:o + n], sb1[:, :n],
                                twb[:, o:o + n], sem="dve")
                gring.done(gs, "mm", lastg)
                uring.done(us, "mm", lastu)
            for oc in range(G):
                db, ds, vd = stream_wcols(ctx, p, ctx.T["mdownT"][e], oc,
                                          FC_M, dring)
                p.wait("tensor", "gg", vd)
                p.wait("tensor", "dve", lastdv)
                lastd = None
                for (o, n) in htl:
                    b = ctx.bank()
                    for fc in range(FC_M):
                        lastd = op(p, "tensor", "matmul", ctx.ps[b][:, :n],
                                   db[:, fc, :], hall[:, fc, o:o + n],
                                   start=(fc == 0), stop=(fc == FC_M - 1),
                                   sem="mm" if fc == FC_M - 1 else None)
                    p.wait("vector", "mm")
                    dv = op(p, "vector", "tensor_add", acc[:, oc, o:o + n],
                            acc[:, oc, o:o + n], ctx.ps[b][:, :n], sem="dve")
                    ctx.bank_done(b, "dve", dv)
                dring.done(ds, "mm", lastd)
        dv = p.cnt["dve"]
        p.wait("sync", "dve", dv)
        for oc in range(G):
            dma(p, "sync", ctx.T["ffn0"][oc * 128:(oc + 1) * 128, h0:h0 + hn],
                acc[:, oc, 0:hn], sem="st")
        p.wait("vector", "st")


@phase("x1ln", ["x_fm", "attn0", "ffn0"], ["x1", "xn1"])
def ph_x1ln(ctx):
    p = ctx.p
    xg = ctx.alloc("ln_xg", (128, NT), F32)
    t1 = ctx.alloc("x1_t1", (128, NT), F32)
    t2 = ctx.alloc("x1_t2", (128, NT), F32)
    for g in range(G):
        r = slice(g * 128, (g + 1) * 128)
        for (o, n) in tiles(TH):
            p.wait_all("sync", ("dve",))
            dma(p, "sync", xg[:, :n], ctx.T["x_fm"][r, J0 + o:J0 + o + n])
            dma(p, "sync", t1[:, :n], ctx.T["attn0"][r, o:o + n])
            lv = dma(p, "sync", t2[:, :n], ctx.T["ffn0"][r, o:o + n])
            p.wait("vector", "ld", lv)
            op(p, "vector", "tensor_add", xg[:, :n], xg[:, :n], t1[:, :n])
            dv = op(p, "vector", "tensor_add", xg[:, :n], xg[:, :n], t2[:, :n],
                    sem="dve")
            p.wait("sync", "dve", dv)
            dma(p, "sync", ctx.T["x1"][r, o:o + n], xg[:, :n], sem="st")
    p.barrier()
    full_ln(ctx, "x1", TH, "xn1")


@phase("conv", ["xn1", "convw_fm", "convb_fm"], ["xc"])
def ph_conv(ctx):
    p = ctx.p
    cw = ctx.alloc("cv_w", (128, G * 4), F32)
    cb = ctx.alloc("cv_b", (128, G), F32)
    xg = ctx.alloc("cv_x", (128, TH), BF16)
    accv = ctx.alloc("cv_a", (128, TM), F32)
    ob = ctx.alloc("cv_o", (128, TM), BF16)
    dma(p, "sync", cw[:, :], ctx.T["convw_fm"])
    lv = dma(p, "sync", cb[:, :], ctx.T["convb_fm"])
    for g in range(G):
        r = slice(g * 128, (g + 1) * 128)
        p.wait_all("sync", ("dve", "act"))
        l2 = dma(p, "sync", xg[:, :], ctx.T["xn1"][r, :])
        p.wait("vector", "ld", max(lv, l2))
        p.wait("vector", "act")
        op(p, "vector", "tensor_scalar_mul", accv[:, :], xg[:, 0:TM],
           cw[:, g * 4:g * 4 + 1])
        for tap in range(1, 4):
            last = op(p, "vector", "scalar_tensor_tensor", accv[:, :],
                      xg[:, tap:tap + TM], cw[:, g * 4 + tap:g * 4 + tap + 1],
                      accv[:, :], ALU.mult, ALU.add,
                      sem="dve" if tap == 3 else None)
        p.wait("scalar", "dve", last)
        p.wait("scalar", "st")
        av = op(p, "scalar", "activation", ob[:, :], accv[:, :], AF.Silu,
                bias=cb[:, g:g + 1], sem="act")
        p.wait("sync", "act", av)
        dma(p, "sync", ctx.T["xc"][r, :], ob[:, :], sem="st")


@phase("l1mm", ["xn1", "xc", "wqT", "wkT", "wvT", "igwT", "ogwT", "gammawT",
                "igb_fm", "ogb_fm", "gammab_bc"],
       ["qh", "kh", "vh", "ig", "og", "afull"])
def ph_l1mm(ctx):
    p = ctx.p
    xr = ctx.alloc("l1_xn", (128, G, TM), BF16)
    xcr = ctx.alloc("l1_xc", (128, G, TM), BF16)
    lv = dma(p, "sync", xr[:, :, :],
             ctx.T["xn1"][:, HALO:TH].rearrange("(g p) t -> p g t", p=128))
    lv = dma(p, "sync", xcr[:, :, :],
             ctx.T["xc"].rearrange("(g p) t -> p g t", p=128))
    igb = ctx.alloc("l1_igb", (128, G), F32)
    ogb = ctx.alloc("l1_ogb", (128, G), F32)
    gmb = ctx.alloc("l1_gmb", (128, H), F32)
    dma(p, "sync", igb[:, :], ctx.T["igb_fm"])
    dma(p, "sync", ogb[:, :], ctx.T["ogb_fm"])
    lv = dma(p, "sync", gmb[:, :], ctx.T["gammab_bc"])
    p.wait("tensor", "ld", lv)
    p.wait("scalar", "ld", lv)
    rf = Ring(ctx, "l1_ef", (128, NT), F32, n=3)
    rb = Ring(ctx, "l1_eb", (128, NT), BF16, n=3)
    for wname, dst in (("wqT", "qh"), ("wkT", "kh"), ("wvT", "vh")):
        proj_stream(ctx, wname, lambda kc, o, n: xr[:, kc, o:o + n], TM,
                    make_copy_evac(ctx, dst, rf))

    def gate_evac(dst, bias):
        def f(oc, o, n, b):
            r = slice(oc * 128, (oc + 1) * 128)
            p.wait("scalar", "mm")
            buf, s = rb.take("scalar")
            ev = op(p, "scalar", "activation", buf[:, :n], ctx.ps[b][:, :n],
                    AF.Sigmoid, bias=bias[:, oc:oc + 1], sem="act")
            ctx.bank_done(b, "act", ev)
            p.wait("sync", "act", ev)
            sv = dma(p, "sync", ctx.T[dst][r, o:o + n], buf[:, :n], sem="st")
            rb.done(s, "st", sv)
        return f

    proj_stream(ctx, "igwT", lambda kc, o, n: xcr[:, kc, o:o + n], TM,
                gate_evac("ig", igb))
    proj_stream(ctx, "ogwT", lambda kc, o, n: xcr[:, kc, o:o + n], TM,
                gate_evac("og", ogb))
    gw = ctx.alloc("l1_gw", (128, G, H), BF16)
    lv = dma(p, "gpsimd", gw[:, :, :],
             ctx.T["gammawT"].rearrange("(g p) h -> p g h", p=128))
    p.wait("tensor", "gg", lv)
    aT = ctx.alloc("l1_aT", (H, TM), F32)
    alog = ctx.alloc("l1_alog", (128, H), F32)
    for ti in range(TM // 128):
        o = ti * 128
        b = ctx.bank()
        for g in range(G):
            op(p, "tensor", "matmul", ctx.ps[b][0:128, 0:H], xcr[:, g, o:o + 128],
               gw[:, g, :], start=(g == 0), stop=(g == G - 1),
               sem="mm" if g == G - 1 else None)
        p.wait("vector", "mm")
        p.wait("vector", "act")
        dv = op(p, "vector", "tensor_add", alog[:, :], ctx.ps[b][0:128, 0:H],
                gmb[:, :], sem="dve")
        ctx.bank_done(b, "dve", dv)
        p.wait("scalar", "dve", dv)
        op(p, "scalar", "activation", alog[:, :], alog[:, :], AF.Sigmoid)
        av = op(p, "scalar", "copy", alog[:, :], alog[:, :], sem="act")
        p.wait("tensor", "act", av)
        b2 = ctx.bank()
        op(p, "tensor", "transpose", ctx.ps[b2][0:H, 0:128], alog[:, 0:H],
           ctx.gl["ident"][:, :], sem="mm")
        p.wait("vector", "mm")
        dv = op(p, "vector", "tensor_copy", aT[0:H, o:o + 128],
                ctx.ps[b2][0:H, 0:128], sem="dve")
        ctx.bank_done(b2, "dve", dv)
    row1 = ctx.alloc("l1_row", (1, TM), F32)
    for h in range(H):
        p.wait_all("sync", ("dve", "ld", "mm"))
        l3 = dma(p, "sync", row1[0:1, :], aT[h:h + 1, :])
        p.wait("tensor", "ld", l3)
        for (o, n) in tiles(TM):
            b = ctx.bank()
            op(p, "tensor", "matmul", ctx.ps[b][:, :n], ctx.gl["ones_r"][0:1, :],
               row1[0:1, o:o + n], start=True, stop=True, sem="mm")
            p.wait("scalar", "mm")
            buf, s = rf.take("scalar")
            ev = op(p, "scalar", "copy", buf[:, :n], ctx.ps[b][:, :n], sem="act")
            ctx.bank_done(b, "act", ev)
            p.wait("sync", "act", ev)
            sv = dma(p, "sync", ctx.T["afull"][h * 128:(h + 1) * 128, o:o + n],
                     buf[:, :n], sem="st")
            rf.done(s, "st", sv)


@phase("norms", ["qh", "kh", "vh", "ig", "vng", "vnb"],
       ["qn", "kn", "vn", "bfull"])
def ph_norms(ctx):
    p = ctx.p
    vng = ctx.alloc("nm_vg", (128, 1), F32)
    vnb = ctx.alloc("nm_vb", (128, 1), F32)
    dma(p, "sync", vng[:, :], ctx.T["vng"])
    lv = dma(p, "sync", vnb[:, :], ctx.T["vnb"])
    xg = ctx.alloc("nm_x", (128, TM), F32)
    t1 = ctx.alloc("nm_t1", (128, TM), F32)
    t2 = ctx.alloc("nm_t2", (128, TM), F32)
    kg = ctx.alloc("nm_k", (128, TM), F32)
    igt = ctx.alloc("nm_ig", (128, TM), BF16)
    rows = ctx.alloc("nm_rows", (1, 3 * TM), F32)
    for g in range(G):
        r = slice(g * 128, (g + 1) * 128)
        for name, dstn in (("qh", "qn"), ("kh", "kn")):
            p.wait_all("sync", ("dve", "act", "st"))
            l2 = dma(p, "sync", xg[:, :], ctx.T[name][r, :])
            p.wait("scalar", "ld", l2)
            rq, _ = head_stats(ctx, xg, TM, rows, False, l2=True)
            bcast_row(ctx, rq, t1, TM)
            p.wait("vector", "act")
            dv = op(p, "vector", "tensor_mul", xg[:, :], xg[:, :], t1[:, :],
                    sem="dve")
            p.wait("sync", "dve", dv)
            dma(p, "sync", ctx.T[dstn][r, :], xg[:, :], sem="st")
            if name == "kh":
                op(p, "vector", "tensor_copy", kg[:, :], xg[:, :], sem="dve")
        p.wait_all("sync", ("dve", "act", "st"))
        l2 = dma(p, "sync", xg[:, :], ctx.T["vh"][r, :])
        p.wait("scalar", "ld", l2)
        rq, rm = head_stats(ctx, xg, TM, rows, True)
        bcast_row(ctx, rq, t1, TM)
        bcast_row(ctx, rm, t2, TM)
        l4 = dma(p, "sync", igt[:, :], ctx.T["ig"][r, :])
        p.wait("vector", "act")
        p.wait("vector", "ld", l4)
        op(p, "vector", "tensor_sub", xg[:, :], xg[:, :], t2[:, :])
        op(p, "vector", "tensor_mul", xg[:, :], xg[:, :], t1[:, :])
        op(p, "vector", "tensor_scalar", xg[:, :], xg[:, :], vng[:, 0:1],
           vnb[:, 0:1], ALU.mult, ALU.add, sem="dve")
        dv = p.cnt["dve"]
        p.wait("sync", "dve", dv)
        dma(p, "sync", ctx.T["vn"][r, :], xg[:, :], sem="st")
        p.wait("vector", "st")
        op(p, "vector", "tensor_mul", xg[:, :], xg[:, :], kg[:, :])
        dv = op(p, "vector", "tensor_mul", xg[:, :], xg[:, :], igt[:, :],
                sem="dve")
        p.wait("sync", "dve", dv)
        dma(p, "sync", ctx.T["bfull"][r, :], xg[:, :], sem="st")


@phase("scan", ["afull", "bfull", "selmask"], ["mem"])
def ph_scan(ctx):
    p = ctx.p
    ag = ctx.alloc("sc_a", (128, TM), F32)
    bg = ctx.alloc("sc_b", (128, TM), F32)
    Bl = ctx.alloc("sc_Bl", (128, TM), F32)
    Btot = ctx.alloc("sc_Bt", (128, G), F32)
    selc = ctx.alloc("sc_sel", (128, 8), F32)
    gath = ctx.alloc("sc_g", (128, 8, G), F32)
    carry = ctx.alloc("sc_c", (128, G), F32)
    lv = dma(p, "sync", selc[:, :], ctx.T["selmask"])
    for g in range(G):
        r = slice(g * 128, (g + 1) * 128)
        p.wait_all("sync", ("dve",))
        dma(p, "sync", ag[:, :], ctx.T["afull"][r, :])
        l2 = dma(p, "sync", bg[:, :], ctx.T["bfull"][r, :])
        p.wait("vector", "ld", l2)
        dv = op(p, "vector", "tensor_tensor_scan", Bl[:, :], ag[:, :],
                bg[:, :], 0.0, ALU.mult, ALU.add, sem="dve")
        p.wait("vector", "dve", dv)
        op(p, "vector", "tensor_copy", Btot[:, g:g + 1], Bl[:, TM - 1:TM],
           sem="dve")
    dv = p.cnt["dve"]
    p.wait("sync", "dve", dv)
    sv = dma(p, "sync", ctx.T["cc_in"].rearrange("o (p g) -> (o p) g", p=128),
             Btot[:, :], sem="st")
    p.wait("gpsimd", "st", sv)
    ccv = p.bump("cc", 1)

    def fcc(e):
        e.collective_compute(
            "AllGather", ALU.bypass, replica_groups=[list(range(8))],
            ins=[ctx.T["cc_in"].opt()], outs=[ctx.T["cc_out"].opt()],
        ).then_inc(_S["cc"], 1)

    p.add("gpsimd", fcc)
    p.wait("gpsimd", "cc", ccv)
    gl = dma(p, "gpsimd", gath[:, :, :],
             ctx.T["cc_out"].rearrange("r (p g) -> p r g", p=128))
    p.wait("vector", "gg", gl)
    p.wait("vector", "ld", lv)
    op(p, "vector", "memset", carry[:, :], 0.0)
    for rk in range(8):
        dv = op(p, "vector", "scalar_tensor_tensor", carry[:, :],
                gath[:, rk, :], selc[:, rk:rk + 1], carry[:, :], ALU.mult,
                ALU.add, sem="dve")
        p.wait("vector", "dve", dv)
    for g in range(G):
        r = slice(g * 128, (g + 1) * 128)
        p.wait_all("sync", ("dve",))
        dma(p, "sync", ag[:, :], ctx.T["afull"][r, :])
        l2 = dma(p, "sync", bg[:, :], ctx.T["bfull"][r, :])
        p.wait("vector", "ld", l2)
        p.wait("vector", "st")
        dv = op(p, "vector", "tensor_tensor_scan", Bl[:, :], ag[:, :], bg[:, :],
                carry[:, g:g + 1], ALU.mult, ALU.add, sem="dve")
        p.wait("vector", "dve", dv)
        p.wait("sync", "dve", dv)
        dma(p, "sync", ctx.T["mem"][r, :], Bl[:, :], sem="st")


@phase("mixout", ["mem", "qn", "og", "mng", "mnb", "gng_fm", "gnb_fm"], ["oo"])
def ph_mixout(ctx):
    p = ctx.p
    mng = ctx.alloc("mx_mg", (128, 1), F32)
    mnb = ctx.alloc("mx_mb", (128, 1), F32)
    gng = ctx.alloc("mx_gg", (128, G), F32)
    gnb = ctx.alloc("mx_gb", (128, G), F32)
    dma(p, "sync", mng[:, :], ctx.T["mng"])
    dma(p, "sync", mnb[:, :], ctx.T["mnb"])
    dma(p, "sync", gng[:, :], ctx.T["gng_fm"])
    lv = dma(p, "sync", gnb[:, :], ctx.T["gnb_fm"])
    xg = ctx.alloc("mx_x", (128, TM), F32)
    t1 = ctx.alloc("mx_t1", (128, TM), F32)
    t2 = ctx.alloc("mx_t2", (128, TM), F32)
    qt = ctx.alloc("mx_q", (128, TM), F32)
    ogt = ctx.alloc("mx_og", (128, TM), BF16)
    ob = ctx.alloc("mx_o", (128, TM), BF16)
    rows = ctx.alloc("mx_rows", (1, 3 * TM), F32)
    for g in range(G):
        r = slice(g * 128, (g + 1) * 128)
        p.wait_all("sync", ("dve", "act", "st"))
        p.wait("vector", "st")
        l2 = dma(p, "sync", xg[:, :], ctx.T["mem"][r, :])
        p.wait("scalar", "ld", l2)
        rq, rm = head_stats(ctx, xg, TM, rows, True)
        bcast_row(ctx, rq, t1, TM)
        bcast_row(ctx, rm, t2, TM)
        l3 = dma(p, "sync", qt[:, :], ctx.T["qn"][r, :])
        p.wait("vector", "act")
        p.wait("vector", "ld", l3)
        op(p, "vector", "tensor_sub", xg[:, :], xg[:, :], t2[:, :])
        op(p, "vector", "tensor_mul", xg[:, :], xg[:, :], t1[:, :])
        op(p, "vector", "tensor_scalar", xg[:, :], xg[:, :], mng[:, 0:1],
           mnb[:, 0:1], ALU.mult, ALU.add)
        dv = op(p, "vector", "tensor_mul", xg[:, :], xg[:, :], qt[:, :],
                sem="dve")
        p.wait("scalar", "dve", dv)
        rq, rm = head_stats(ctx, xg, TM, rows, True)
        bcast_row(ctx, rq, t1, TM)
        bcast_row(ctx, rm, t2, TM)
        l4 = dma(p, "sync", ogt[:, :], ctx.T["og"][r, :])
        p.wait("vector", "act")
        p.wait("vector", "ld", l4)
        op(p, "vector", "tensor_sub", xg[:, :], xg[:, :], t2[:, :])
        op(p, "vector", "tensor_mul", xg[:, :], xg[:, :], t1[:, :])
        op(p, "vector", "tensor_scalar", xg[:, :], xg[:, :], gng[:, g:g + 1],
           gnb[:, g:g + 1], ALU.mult, ALU.add)
        dv = op(p, "vector", "tensor_mul", ob[:, :], xg[:, :], ogt[:, :],
                sem="dve")
        p.wait("sync", "dve", dv)
        dma(p, "sync", ctx.T["oo"][r, :], ob[:, :], sem="st")


@phase("wo", ["oo", "woT"], ["attn1"])
def ph_wo(ctx):
    p = ctx.p
    yr = ctx.alloc("wq_o", (128, G, TM), BF16)
    lv = dma(p, "sync", yr[:, :, :],
             ctx.T["oo"].rearrange("(g p) t -> p g t", p=128))
    p.wait("tensor", "ld", lv)
    ring = Ring(ctx, "wq_ev", (128, NT), F32, n=3)
    proj_stream(ctx, "woT", lambda kc, o, n: yr[:, kc, o:o + n], TM,
                make_copy_evac(ctx, "attn1", ring))


@phase("ffn", ["xn1", "qup", "qgate", "qdown", "qscales", "x1", "attn1"],
       ["out_fm"])
def ph_ffn(ctx):
    p = ctx.p
    rows = ctx.alloc("ff_rows", (1, 16), F32)
    lv = dma(p, "sync", rows[0:1, :], ctx.T["qscales"][0:1, :])
    p.wait("vector", "ld", lv)
    op(p, "vector", "tensor_mul", rows[0:1, 12:13], rows[0:1, 5:6],
       rows[0:1, 6:7], sem="dve")
    supc = ctx.alloc("ff_su", (128, 1), F32)
    sgdc = ctx.alloc("ff_sg", (128, 1), F32)
    bcast_row(ctx, rows[0:1, 4:5], supc, 1)
    bcast_row(ctx, rows[0:1, 12:13], sgdc, 1)
    p.wait("scalar", "act")
    xr = ctx.alloc("ff_xn", (128, G, TM), BF16)
    lv = dma(p, "sync", xr[:, :, :],
             ctx.T["xn1"][:, HALO:TH].rearrange("(g p) t -> p g t", p=128))
    p.wait("tensor", "ld", lv)
    hall = ctx.alloc("ff_h", (128, FC_F, NT), BF16)
    uring = Ring(ctx, "ff_wu", (128, G, 128), BF16, n=2)
    gring = Ring(ctx, "ff_wg", (128, G, 128), BF16, n=2)
    dring = Ring(ctx, "ff_wd", (128, FC_F, 128), BF16, n=2)
    sb1 = ctx.alloc("ff_e1", (128, NT), BF16)
    sb2 = ctx.alloc("ff_e2", (128, NT), BF16)
    fx1 = ctx.alloc("ff_x1", (128, NT), F32)
    fat = ctx.alloc("ff_at", (128, NT), F32)
    oring = Ring(ctx, "ff_o", (128, NT), F32, n=3)
    for (o, n) in tiles(TM):
        lastdv = None
        for fc in range(FC_F):
            ub, us, v1 = stream_wcols(ctx, p, ctx.T["qup"], fc, G, uring)
            gb, gs, v2 = stream_wcols(ctx, p, ctx.T["qgate"], fc, G, gring)
            p.wait("tensor", "gg", v2)
            bu = ctx.bank()
            bg = ctx.bank()
            lastu = lastg = None
            for g in range(G):
                lastu = op(p, "tensor", "matmul", ctx.ps[bu][:, :n],
                           ub[:, g, :], xr[:, g, o:o + n], start=(g == 0),
                           stop=(g == G - 1), sem="mm" if g == G - 1 else None)
            for g in range(G):
                lastg = op(p, "tensor", "matmul", ctx.ps[bg][:, :n],
                           gb[:, g, :], xr[:, g, o:o + n], start=(g == 0),
                           stop=(g == G - 1), sem="mm" if g == G - 1 else None)
            uring.done(us, "mm", lastu)
            gring.done(gs, "mm", lastg)
            p.wait("scalar", "mm")
            p.wait("scalar", "dve")
            op(p, "scalar", "activation", sb1[:, :n], ctx.ps[bu][:, :n],
               AF.Silu, scale=supc[:, 0:1])
            av = op(p, "scalar", "copy", sb2[:, :n], ctx.ps[bg][:, :n],
                    sem="act")
            ctx.bank_done(bu, "act", av)
            ctx.bank_done(bg, "act", av)
            p.wait("vector", "act", av)
            lastdv = op(p, "vector", "tensor_mul", hall[:, fc, :n], sb1[:, :n],
                        sb2[:, :n], sem="dve")
        for oc in range(G):
            db, ds, vd = stream_wcols(ctx, p, ctx.T["qdown"], oc, FC_F, dring)
            p.wait("tensor", "gg", vd)
            p.wait("tensor", "dve", lastdv)
            b = ctx.bank()
            lastd = None
            for fc in range(FC_F):
                lastd = op(p, "tensor", "matmul", ctx.ps[b][:, :n],
                           db[:, fc, :], hall[:, fc, :n], start=(fc == 0),
                           stop=(fc == FC_F - 1),
                           sem="mm" if fc == FC_F - 1 else None)
            dring.done(ds, "mm", lastd)
            rr = slice(oc * 128, (oc + 1) * 128)
            p.wait_all("sync", ("dve",))
            dma(p, "sync", fx1[:, :n], ctx.T["x1"][rr, HALO + o:HALO + o + n])
            l2 = dma(p, "sync", fat[:, :n], ctx.T["attn1"][rr, o:o + n])
            p.wait("scalar", "mm")
            obuf, osl = oring.take("scalar")
            ev = op(p, "scalar", "activation", obuf[:, :n], ctx.ps[b][:, :n],
                    AF.Copy, scale=sgdc[:, 0:1], sem="act")
            ctx.bank_done(b, "act", ev)
            p.wait("vector", "act", ev)
            p.wait("vector", "ld", l2)
            op(p, "vector", "tensor_add", obuf[:, :n], obuf[:, :n], fx1[:, :n])
            dv = op(p, "vector", "tensor_add", obuf[:, :n], obuf[:, :n],
                    fat[:, :n], sem="dve")
            p.wait("sync", "dve", dv)
            sv = dma(p, "sync", ctx.T["out_fm"][rr, o:o + n], obuf[:, :n],
                     sem="st")
            oring.done(osl, "st", sv)


ALL_PHASES = [n for (n, _, _, _) in PHASES]
LAUNCH_GROUPS = [ALL_PHASES]


def _needed_later(name, group, specs):
    if specs[name][2] == "out":
        return True
    idx = max(ALL_PHASES.index(n) for n in group)
    for (n, r, w, _) in PHASES:
        if ALL_PHASES.index(n) > idx and name in r:
            return True
    return False


def build_launch(group, dbg_outputs=()):
    specs = tensor_specs()
    ph = {n: (r, w, f) for (n, r, w, f) in PHASES}
    reads, writes = set(), set()
    for n in group:
        r, w, _ = ph[n]
        reads.update(r)
        writes.update(w)
    nc = bass.Bass("TRN2", target_bir_lowering=False)
    T, innames, outnames = {}, [], []
    for name in sorted(reads | writes):
        shape, dt, kind = specs[name]
        if kind == "cc":
            continue
        if name in writes and (_needed_later(name, group, specs)
                               or name in dbg_outputs):
            T[name] = nc.dram_tensor(name, shape, dt, kind="ExternalOutput").ap()
            outnames.append(name)
        elif name in reads and name not in writes:
            T[name] = nc.dram_tensor(name, shape, dt, kind="ExternalInput").ap()
            innames.append(name)
        else:
            T[name] = nc.dram_tensor(name, shape, dt).ap()
    if "scan" in group:
        for name in ("cc_in", "cc_out"):
            shape, dt, _ = specs[name]
            T[name] = nc.dram_tensor(name, shape, dt).ap()

    p = Prog()
    es = ExitStack()
    with es:
        psum = [es.enter_context(nc.psum_tensor(f"pb{i}", [128, NT], F32))
                for i in range(8)]
        for s in SEMS:
            _S[s] = es.enter_context(nc.semaphore(s))
        ctx = Ctx(nc, p, T, psum)
        gstack = es.enter_context(ExitStack())
        ctx.stack = gstack
        consts(ctx)
        p.barrier()
        for n in group:
            _, _, f = ph[n]
            with ExitStack() as phstack:
                ctx.stack = phstack
                ctx.cache = {}
                f(ctx)
                p.barrier()
        p.wait("sync", "st")
        p.wait("sync", "gg")
        p.wait("sync", "ld")
        with nc.Block() as blk:
            for eng in ENGINES:
                fns = p.ops[eng]
                if not fns:
                    continue

                def make(fns):
                    def _(e):
                        for f in fns:
                            f(e)
                    return _
                getattr(blk, eng)(make(fns))
    return nc, innames, outnames


def fm(vec):
    return np.ascontiguousarray(
        np.asarray(vec, np.float32).reshape(G, 128).T)


def prep_inputs(inp):
    x = np.asarray(inp["L0_attn_x"], np.float32)
    B = x.shape[0]
    xpad = np.concatenate([np.zeros((B, WARM, C), np.float32), x], axis=1)
    tr = lambda a: np.ascontiguousarray(np.asarray(a, np.float32).swapaxes(-1, -2))
    sh = {
        "wrvkT": tr(inp["L0_attn_W_rvk"]),
        "woutT": tr(inp["L0_attn_W_out"]),
        "td_fm": fm(inp["L0_attn_time_decay"]),
        "tf_fm": fm(inp["L0_attn_time_first"]),
        "gatewT": tr(inp["L0_ffn_gate_w"]),
        "mgateT": tr(inp["L0_ffn_gate_proj"]),
        "mupT": tr(inp["L0_ffn_up_proj"]),
        "mdownT": tr(inp["L0_ffn_down_proj"]),
        "wqT": tr(inp["L1_attn_Wq"]),
        "wkT": tr(inp["L1_attn_Wk"]),
        "wvT": tr(inp["L1_attn_Wv"]),
        "woT": tr(inp["L1_attn_Wo"]),
        "igwT": tr(inp["L1_attn_ig_w"]),
        "igb_fm": fm(inp["L1_attn_ig_b"]),
        "ogwT": tr(inp["L1_attn_og_w"]),
        "ogb_fm": fm(inp["L1_attn_og_b"]),
        "gammawT": tr(inp["L1_attn_gamma_w"]),
        "gammab_bc": np.ascontiguousarray(np.broadcast_to(
            np.asarray(inp["L1_attn_gamma_b"], np.float32)[None, :], (128, H))),
        "vng": np.asarray(inp["L1_attn_vn_g"], np.float32).reshape(128, 1),
        "vnb": np.asarray(inp["L1_attn_vn_b"], np.float32).reshape(128, 1),
        "mng": np.asarray(inp["L1_attn_mn_g"], np.float32).reshape(128, 1),
        "mnb": np.asarray(inp["L1_attn_mn_b"], np.float32).reshape(128, 1),
        "gng_fm": fm(inp["L1_attn_gn_g"]),
        "gnb_fm": fm(inp["L1_attn_gn_b"]),
        "fupT": tr(inp["L1_ffn_W_up"]),
        "fgateT": tr(inp["L1_ffn_W_gate"]),
        "fdownT": tr(inp["L1_ffn_W_down"]),
        "convb_fm": fm(inp["L1_attn_conv_b"]),
    }
    cw = np.asarray(inp["L1_attn_conv_w"], np.float32)[:, 0, :]
    sh["convw_fm"] = np.ascontiguousarray(
        cw.reshape(G, 128, 4).transpose(1, 0, 2).reshape(128, G * 4))
    percore = []
    for c in range(8):
        b, half = c // 2, c % 2
        t0 = half * TM
        d = {
            "x_fm": np.ascontiguousarray(xpad[b, t0:t0 + TW, :].T),
            "wmask": np.full((128, 1), 1.0 if half else 0.0, np.float32),
        }
        sel = np.zeros((128, 8), np.float32)
        if half == 1:
            sel[:, c - 1] = 1.0
        d["selmask"] = sel
        percore.append(d)
    return sh, percore


def kernel(**inputs):
    shared, percore = prep_inputs(inputs)
    carry = [dict(pc) for pc in percore]
    for group in LAUNCH_GROUPS:
        nc, innames, outnames = build_launch(group)
        in_maps = []
        for c in range(8):
            m = {}
            for n in innames:
                m[n] = carry[c][n] if n in carry[c] else shared[n]
            in_maps.append(m)
        res = run_bass_kernel_spmd(nc, in_maps, core_ids=list(range(8)))
        for c in range(8):
            for n in outnames:
                carry[c][n] = np.asarray(res.results[c][n])
    x = np.asarray(inputs["L0_attn_x"], np.float32)
    B, T, _ = x.shape
    outa = np.empty((B, T, C), np.float32)
    for c in range(8):
        b, half = c // 2, c % 2
        outa[b, half * TM:(half + 1) * TM, :] = carry[c]["out_fm"].T
    return outa


# version 07 (final)
# speedup vs baseline: 1.0989x; 1.0989x over previous
from contextlib import ExitStack

import numpy as np
import ml_dtypes
import concourse.bass as bass
import concourse.mybir as mybir
from concourse.bass_utils import run_bass_kernel_spmd

F32 = mybir.dt.float32
BF16 = mybir.dt.bfloat16
AF = mybir.ActivationFunctionType
ALU = mybir.AluOpType
AX = mybir.AxisListType
bfnp = ml_dtypes.bfloat16

C = 2048
G = C // 128
H = 16
WARM = 256
HALO = 3
E = 8
FME = 1408
FC_M = FME // 128
DF = 8192
FC_F = DF // 128
NT = 512

TM = 2048
TH = TM + HALO
TW = TM + WARM
J0 = WARM - HALO
LN_EPS = 1e-5


def tiles(total, n=NT):
    k = (total + n - 1) // n
    base = total // k
    rem = total - base * k
    out, off = [], 0
    for i in range(k):
        sz = base + (1 if i < rem else 0)
        out.append((off, sz))
        off += sz
    return out


ENGINES = ("sync", "gpsimd", "tensor", "vector", "scalar")
SEMS = ("ld", "gg", "mm", "act", "dve", "cc", "st")
_S = {}


class Prog:
    def __init__(self):
        self.ops = {e: [] for e in ENGINES}
        self.cnt = {s: 0 for s in SEMS}

    def add(self, eng, fn):
        self.ops[eng].append(fn)

    def bump(self, sem, by):
        self.cnt[sem] += by
        return self.cnt[sem]

    def wait(self, eng, sem, val=None):
        v = self.cnt[sem] if val is None else val
        if not v:
            return
        self.add(eng, lambda e, s=sem, v=v: e.wait_ge(_S[s], v))

    def wait_all(self, eng, sems=("ld", "gg", "mm", "act", "dve", "st")):
        for s in sems:
            self.wait(eng, s)

    def barrier(self):
        for e in ENGINES:
            self.wait_all(e, SEMS[:-2] + ("st",))


def dma(p, q, out, in_, sem=None):
    if sem is None:
        sem = "ld" if q == "sync" else "gg"
    v = p.bump(sem, 16)

    def f(e, out=out, in_=in_, sem=sem):
        e.dma_start(out=out, in_=in_).then_inc(_S[sem], 16)

    p.add(q, f)
    return v


def op(p, eng, meth, *args, sem=None, **kw):
    v = p.bump(sem, 1) if sem else None
    for k in ("bias", "scale", "alpha"):
        if k in kw and isinstance(kw[k], float) and kw[k] != 0.0:
            kw[k] = np.float32(kw[k])

    def f(e, meth=meth, args=args, kw=kw, sem=sem):
        inst = getattr(e, meth)(*args, **kw)
        if sem:
            inst.then_inc(_S[sem], 1)

    p.add(eng, f)
    return v


class Ctx:
    def __init__(self, nc, p, T, psum):
        self.nc = nc
        self.p = p
        self.T = T
        self.ps = psum
        self.ps_guard = [None] * len(psum)
        self.ps_next = 0
        self.stack = None
        self.cache = {}
        self.gl = {}

    def alloc(self, name, shape, dt):
        key = name
        if key in self.cache:
            return self.cache[key]
        if not hasattr(self, "_names"):
            self._names = {}
        k = self._names.get(name, 0)
        self._names[name] = k + 1
        uname = name if k == 0 else f"{name}__{k}"
        t = self.stack.enter_context(self.nc.sbuf_tensor(uname, list(shape), dt))
        self.cache[key] = t
        return t

    def bank(self, nbanks_away=0):
        i = self.ps_next
        self.ps_next = (self.ps_next + 1) % len(self.ps)
        g = self.ps_guard[i]
        if g:
            self.p.wait("tensor", g[0], g[1])
            self.ps_guard[i] = None
        return i

    def bank_done(self, i, sem, val):
        self.ps_guard[i] = (sem, val)


class Ring:

    def __init__(self, ctx, name, shape, dt, n=4):
        self.bufs = [ctx.alloc(f"{name}_{i}", shape, dt) for i in range(n)]
        self.guard = [None] * n
        self.i = 0
        self.p = ctx.p

    def take(self, eng):
        i = self.i
        self.i = (self.i + 1) % len(self.bufs)
        if self.guard[i]:
            s, v = self.guard[i]
            self.p.wait(eng, s, v)
            self.guard[i] = None
        return self.bufs[i], i

    def done(self, i, sem, val):
        self.guard[i] = (sem, val)


def tensor_specs():
    sp = {}
    A = sp.__setitem__
    A("x_fm", ([C, TW], F32, "in"))
    A("wmask", ([128, 1], F32, "in"))
    A("selmask", ([128, 8], F32, "in"))
    A("td_fm", ([128, G], F32, "in"))
    A("tf_fm", ([128, G], F32, "in"))
    A("wrvkT", ([C, 3 * C], F32, "in"))
    A("woutT", ([C, C], F32, "in"))
    A("gatewT", ([C, E], F32, "in"))
    A("mgateT", ([E, C, FME], F32, "in"))
    A("mupT", ([E, C, FME], F32, "in"))
    A("mdownT", ([E, FME, C], F32, "in"))
    A("wqT", ([C, C], F32, "in"))
    A("wkT", ([C, C], F32, "in"))
    A("wvT", ([C, C], F32, "in"))
    A("woT", ([C, C], F32, "in"))
    A("convw_fm", ([128, G * 4], F32, "in"))
    A("convb_fm", ([128, G], F32, "in"))
    A("igwT", ([C, C], F32, "in"))
    A("igb_fm", ([128, G], F32, "in"))
    A("ogwT", ([C, C], F32, "in"))
    A("ogb_fm", ([128, G], F32, "in"))
    A("gammawT", ([C, H], F32, "in"))
    A("gammab_bc", ([128, H], F32, "in"))
    A("vng", ([128, 1], F32, "in"))
    A("vnb", ([128, 1], F32, "in"))
    A("mng", ([128, 1], F32, "in"))
    A("mnb", ([128, 1], F32, "in"))
    A("gng_fm", ([128, G], F32, "in"))
    A("gnb_fm", ([128, G], F32, "in"))
    A("fupT", ([C, DF], F32, "in"))
    A("fgateT", ([C, DF], F32, "in"))
    A("fdownT", ([DF, C], F32, "in"))
    A("xn", ([C, TW], BF16, "s"))
    A("sr", ([C, TW], BF16, "s"))
    A("vsc", ([C, TW], F32, "s"))
    A("ek", ([C, TW], F32, "s"))
    A("ekv", ([C, TW], F32, "s"))
    A("y", ([C, TH], BF16, "s"))
    A("attn0", ([C, TH], F32, "s"))
    A("tw", ([E * 128, TH], BF16, "s"))
    A("ffn0", ([C, TH], F32, "s"))
    A("x1", ([C, TH], F32, "s"))
    A("xn1", ([C, TH], BF16, "s"))
    A("xc", ([C, TM], BF16, "s"))
    A("qh", ([C, TM], F32, "s"))
    A("kh", ([C, TM], F32, "s"))
    A("vh", ([C, TM], F32, "s"))
    A("qn", ([C, TM], F32, "s"))
    A("kn", ([C, TM], F32, "s"))
    A("vn", ([C, TM], F32, "s"))
    A("ig", ([C, TM], BF16, "s"))
    A("og", ([C, TM], BF16, "s"))
    A("afull", ([C, TM], F32, "s"))
    A("bfull", ([C, TM], F32, "s"))
    A("mem", ([C, TM], F32, "s"))
    A("oo", ([C, TM], BF16, "s"))
    A("attn1", ([C, TM], F32, "s"))
    A("qscales", ([1, 16], F32, "s"))
    A("qup", ([C, DF], BF16, "s"))
    A("qgate", ([C, DF], BF16, "s"))
    A("qdown", ([DF, C], BF16, "s"))
    A("cc_in", ([1, C], F32, "cc"))
    A("cc_out", ([8, C], F32, "cc"))
    A("out_fm", ([C, TM], F32, "out"))
    return sp


def consts(ctx):
    p, nc = ctx.p, ctx.nc
    oc_ = ctx.gl["ones_c"] = ctx.alloc("g_ones_c", (128, 1), BF16)
    or_ = ctx.gl["ones_r"] = ctx.alloc("g_ones_r", (1, 128), F32)
    id_ = ctx.gl["ident"] = ctx.alloc("g_ident", (128, 128), F32)
    a = nc.inline_tensor(np.ones((128, 1), dtype=bfnp), "c_ones_c")
    b = nc.inline_tensor(np.ones((1, 128), dtype=np.float32), "c_ones_r")
    c = nc.inline_tensor(np.eye(128, dtype=np.float32), "c_ident")
    dma(p, "sync", oc_[:, :], a.ap())
    dma(p, "sync", or_[:, :], b.ap())
    v = dma(p, "sync", id_[:, :], c.ap())
    for e in ("tensor", "vector", "scalar"):
        p.wait(e, "ld", v)


def bcast_row(ctx, row_ap, dst_ap, n):
    p = ctx.p
    p.wait("tensor", "act")
    p.wait("tensor", "dve")
    for (o, sz) in tiles(n):
        b = ctx.bank()
        op(p, "tensor", "matmul", ctx.ps[b][:, :sz], ctx.gl["ones_r"][0:1, :],
           row_ap[0:1, o:o + sz], start=True, stop=True, sem="mm")
        p.wait("scalar", "mm")
        if sz >= 128:
            ev = op(p, "scalar", "copy", dst_ap[:, o:o + sz], ctx.ps[b][:, :sz],
                    sem="act")
        else:
            op(p, "scalar", "copy", dst_ap[:, o:o + sz], ctx.ps[b][:, :sz])
            ev = op(p, "scalar", "copy", dst_ap[:, o:o + sz],
                    dst_ap[:, o:o + sz], sem="act")
        ctx.bank_done(b, "act", ev)


def full_ln(ctx, src_name, Tlen, dst_name):
    p = ctx.p
    src, dst = ctx.T[src_name], ctx.T[dst_name]
    xg = ctx.alloc("ln_xg", (128, NT), F32)
    xb = ctx.alloc("ln_xb", (128, NT), BF16)
    sq = ctx.alloc("ln_sq", (128, NT), BF16)
    t1 = ctx.alloc("ln_t1", (128, NT), F32)
    rows = ctx.alloc("ln_rows", (1, 4 * TW), F32)
    rb = ctx.alloc("ln_rb", (128, TW), F32)
    mrb = ctx.alloc("ln_mrb", (128, TW), F32)
    tl = tiles(Tlen)
    for (o, n) in tl:
        bs = ctx.bank()
        bq = ctx.bank()
        for g in range(G):
            p.wait_all("sync", ("act",))
            lv = dma(p, "sync", xg[:, :n], src[g * 128:(g + 1) * 128, o:o + n])
            p.wait("scalar", "ld", lv)
            p.wait("scalar", "mm")
            op(p, "scalar", "copy", xb[:, :n], xg[:, :n])
            av = op(p, "scalar", "activation", sq[:, :n], xg[:, :n], AF.Square,
                    sem="act")
            p.wait("tensor", "act", av)
            op(p, "tensor", "matmul", ctx.ps[bs][0:1, :n],
               ctx.gl["ones_c"][:, 0:1], xb[:, :n], start=(g == 0),
               stop=(g == G - 1), sem="mm" if g == G - 1 else None)
            op(p, "tensor", "matmul", ctx.ps[bq][0:1, :n],
               ctx.gl["ones_c"][:, 0:1], sq[:, :n], start=(g == 0),
               stop=(g == G - 1), sem="mm" if g == G - 1 else None)
        p.wait("scalar", "mm")
        ev = op(p, "scalar", "copy", rows[0:1, o:o + n], ctx.ps[bs][0:1, :n],
                sem="act")
        e2 = op(p, "scalar", "copy", rows[0:1, TW + o:TW + o + n],
                ctx.ps[bq][0:1, :n], sem="act")
        ctx.bank_done(bs, "act", ev)
        ctx.bank_done(bq, "act", e2)
    sA = rows[0:1, 0:Tlen]
    sQ = rows[0:1, TW:TW + Tlen]
    rR = rows[0:1, 2 * TW:2 * TW + Tlen]
    rM = rows[0:1, 3 * TW:3 * TW + Tlen]
    p.wait("vector", "act")
    op(p, "vector", "tensor_scalar_mul", sA, sA, 1.0 / C)
    op(p, "vector", "tensor_scalar_mul", sQ, sQ, 1.0 / C)
    op(p, "vector", "tensor_mul", rR, sA, sA)
    op(p, "vector", "tensor_sub", sQ, sQ, rR, sem="dve")
    p.wait("scalar", "dve")
    op(p, "scalar", "activation", rR, sQ, AF.Ln, bias=LN_EPS)
    op(p, "scalar", "activation", rR, rR, AF.Exp, scale=-0.5, sem="act")
    p.wait("vector", "act")
    op(p, "vector", "tensor_mul", rM, sA, rR, sem="dve")
    bcast_row(ctx, rR, rb, Tlen)
    bcast_row(ctx, rM, mrb, Tlen)
    for g in range(G):
        for (o, n) in tl:
            p.wait_all("sync", ("dve",))
            lv = dma(p, "sync", xg[:, :n], src[g * 128:(g + 1) * 128, o:o + n])
            p.wait("vector", "ld", lv)
            p.wait("vector", "act")
            p.wait("vector", "st")
            op(p, "vector", "tensor_mul", t1[:, :n], xg[:, :n], rb[:, o:o + n])
            dv = op(p, "vector", "tensor_sub", xb[:, :n], t1[:, :n],
                    mrb[:, o:o + n], sem="dve")
            p.wait("sync", "dve", dv)
            dma(p, "sync", dst[g * 128:(g + 1) * 128, o:o + n], xb[:, :n],
                sem="st")


def head_stats(ctx, xg_ap, n, rows, want_mean, l2=False, eps=LN_EPS):
    p = ctx.p
    xb = ctx.alloc("hs_xb", (128, NT), BF16)
    sq = ctx.alloc("hs_sq", (128, NT), BF16)
    rQ = rows[0:1, 0:n]
    rS = rows[0:1, TM:TM + n]
    r2 = rows[0:1, 2 * TM:2 * TM + n]
    for (o, sz) in tiles(n):
        bq = ctx.bank()
        bs = ctx.bank() if want_mean else None
        p.wait("scalar", "mm")
        av = op(p, "scalar", "activation", sq[:, :sz], xg_ap[:, o:o + sz],
                AF.Square, sem="act")
        if want_mean:
            av = op(p, "scalar", "copy", xb[:, :sz], xg_ap[:, o:o + sz],
                    sem="act")
        p.wait("tensor", "act", av)
        mv = op(p, "tensor", "matmul", ctx.ps[bq][0:1, :sz],
                ctx.gl["ones_c"][:, 0:1], sq[:, :sz], start=True, stop=True,
                sem="mm")
        if want_mean:
            mv = op(p, "tensor", "matmul", ctx.ps[bs][0:1, :sz],
                    ctx.gl["ones_c"][:, 0:1], xb[:, :sz], start=True,
                    stop=True, sem="mm")
        p.wait("vector", "mm", mv)
        dv = op(p, "vector", "tensor_scalar_mul", rQ[0:1, o:o + sz],
                ctx.ps[bq][0:1, :sz], 1.0 if l2 else 1.0 / 128, sem="dve")
        ctx.bank_done(bq, "dve", dv)
        if want_mean:
            dv = op(p, "vector", "tensor_scalar_mul", rS[0:1, o:o + sz],
                    ctx.ps[bs][0:1, :sz], 1.0 / 128, sem="dve")
            ctx.bank_done(bs, "dve", dv)
    if want_mean:
        op(p, "vector", "tensor_mul", r2, rS, rS)
        op(p, "vector", "tensor_sub", rQ, rQ, r2, sem="dve")
    if l2:
        op(p, "vector", "tensor_scalar_max", rQ, rQ, 1e-24, sem="dve")
    p.wait("scalar", "dve")
    op(p, "scalar", "activation", rQ, rQ, AF.Ln,
       bias=0.0 if l2 else eps)
    op(p, "scalar", "activation", rQ, rQ, AF.Exp, scale=-0.5, sem="act")
    return rQ, rS


def stream_wcols(ctx, p, wap, oc, kgroups, wring):
    buf, slot = wring.take("gpsimd")
    v = dma(p, "gpsimd", buf[:, :, :],
            wap[:, oc * 128:(oc + 1) * 128].rearrange("(kc p) m -> p kc m",
                                                      p=128))
    return buf, slot, v


def proj_stream(ctx, wname, src_res_fn, Tlen, evac, ocs=G, kgroups=G, woff=0):
    p = ctx.p
    W = ctx.T[wname]
    wring = Ring(ctx, f"wr_{wname}", (128, kgroups, 128), BF16, n=2)
    tl = tiles(Tlen)
    for oc in range(ocs):
        buf, slot, gv = stream_wcols(ctx, p, W[:, woff:woff + ocs * 128], oc,
                                     kgroups, wring)
        p.wait("tensor", "gg", gv)
        last = None
        for ti, (o, n) in enumerate(tl):
            b = ctx.bank()
            for kc in range(kgroups):
                last = op(p, "tensor", "matmul", ctx.ps[b][:, :n],
                          buf[:, kc, :], src_res_fn(kc, o, n),
                          start=(kc == 0), stop=(kc == kgroups - 1),
                          sem="mm" if kc == kgroups - 1 else None)
            evac(oc, o, n, b)
        wring.done(slot, "mm", last)


PHASES = []


def phase(name, reads, writes):
    def deco(fn):
        PHASES.append((name, reads, writes, fn))
        return fn
    return deco


@phase("ln0", ["x_fm"], ["xn"])
def ph_ln0(ctx):
    full_ln(ctx, "x_fm", TW, "xn")


@phase("qscale", ["fupT", "fgateT", "fdownT"], ["qscales"])
def ph_qscale(ctx):
    p = ctx.p
    wq = ctx.alloc("qs_w", (128, DF), BF16)
    rows = ctx.alloc("qs_rows", (1, 16), F32)
    for wi, wname in enumerate(("fupT", "fgateT", "fdownT")):
        W = ctx.T[wname]
        R, Cc = tensor_specs()[wname][0]
        b = ctx.bank()
        first = True
        for r0 in range(0, R, 128):
            p.wait_all("gpsimd", ("mm",))
            lv = dma(p, "gpsimd", wq[:, :Cc], W[r0:r0 + 128, :])
            p.wait("vector", "gg", lv)
            p.wait("vector", "mm")
            dv = op(p, "vector", "tensor_single_scalar",
                    wq[:, :Cc].bitcast(mybir.dt.uint16),
                    wq[:, :Cc].bitcast(mybir.dt.uint16), 0x7FFF,
                    ALU.bitwise_and, sem="dve")
            p.wait("tensor", "dve", dv)
            tl_q = tiles(Cc)
            for (o, n) in tl_q:
                lastmm = (r0 + 128 >= R) and (o + n >= Cc)
                chunk_last = (o + n >= Cc)
                op(p, "tensor", "matmul", ctx.ps[b][0:1, 0:n],
                   ctx.gl["ones_c"][:, 0:1], wq[:, o:o + n],
                   start=first, stop=lastmm,
                   sem="mm" if chunk_last else None)
                first = False
        p.wait("vector", "mm")
        dv = op(p, "vector", "tensor_reduce", rows[0:1, wi:wi + 1],
                ctx.ps[b][0:1, 0:NT], AX.X, ALU.add, sem="dve")
        p.wait("vector", "dve", dv)
        dv = op(p, "vector", "tensor_scalar", rows[0:1, 4 + wi:5 + wi],
                rows[0:1, wi:wi + 1], 1.0 / (R * Cc), 1e-8, ALU.mult, ALU.add,
                sem="dve")
        p.wait("vector", "dve", dv)
        ctx.bank_done(b, "dve", dv)
    for wi in range(3):
        dv = op(p, "vector", "tensor_scalar_mul", rows[0:1, 8 + wi:9 + wi],
                rows[0:1, 4 + wi:5 + wi], 0.5, sem="dve")
        p.wait("vector", "dve", dv)
    dv = op(p, "vector", "tensor_copy", rows[0:1, 11:12], rows[0:1, 4:5],
            sem="dve")
    p.wait("sync", "dve", dv)
    dma(p, "sync", ctx.T["qscales"][0:1, 0:16], rows[0:1, 0:16], sem="st")


@phase("qquant", ["fupT", "fgateT", "fdownT", "qscales"],
       ["qup", "qgate", "qdown"])
def ph_qquant(ctx):
    p = ctx.p
    rows = ctx.alloc("qq_rows", (1, 16), F32)
    thrn = ctx.alloc("qq_thrn", (1, 4), F32)
    lv = dma(p, "sync", rows[0:1, 0:16], ctx.T["qscales"][0:1, 0:16])
    p.wait("vector", "ld", lv)
    for wi in range(3):
        op(p, "vector", "tensor_scalar_mul", thrn[0:1, wi:wi + 1],
           rows[0:1, 8 + wi:9 + wi], -1.0, sem="dve" if wi == 2 else None)
    thrc = ctx.alloc("qq_thrc", (128, 4), F32)
    thrcn = ctx.alloc("qq_thrcn", (128, 4), F32)
    for wi in range(3):
        bcast_row(ctx, rows[0:1, 8 + wi:9 + wi], thrc[:, wi:wi + 1], 1)
        bcast_row(ctx, thrn[0:1, wi:wi + 1], thrcn[:, wi:wi + 1], 1)
    t1 = ctx.alloc("qq_t1", (128, DF), F32)
    wring = Ring(ctx, "qq_wf", (128, DF), F32, n=2)
    qring = Ring(ctx, "qq_qb", (128, DF), BF16, n=2)
    for wi, (wname, qname) in enumerate(
            (("fupT", "qup"), ("fgateT", "qgate"), ("fdownT", "qdown"))):
        W, Q = ctx.T[wname], ctx.T[qname]
        R, Cc = tensor_specs()[wname][0]
        for r0 in range(0, R, 128):
            wf, ws_ = wring.take("sync")
            lv = dma(p, "sync", wf[:, :Cc], W[r0:r0 + 128, :])
            p.wait("vector", "ld", lv)
            op(p, "vector", "tensor_single_scalar", t1[:, :Cc], wf[:, :Cc],
               thrc[:, wi:wi + 1], ALU.is_ge)
            op(p, "vector", "tensor_single_scalar", wf[:, :Cc], wf[:, :Cc],
               thrcn[:, wi:wi + 1], ALU.is_le)
            qb, qs_ = qring.take("vector")
            dv = op(p, "vector", "tensor_sub", qb[:, :Cc], t1[:, :Cc],
                    wf[:, :Cc], sem="dve")
            wring.done(ws_, "dve", dv)
            p.wait("sync", "dve", dv)
            sv = dma(p, "sync", Q[r0:r0 + 128, :], qb[:, :Cc], sem="st")
            qring.done(qs_, "st", sv)


@phase("rvk", ["xn", "wrvkT"], ["sr", "vsc", "ek", "ekv"])
def ph_rvk(ctx):
    p = ctx.p
    xr = ctx.alloc("rvk_xn", (128, G, TW), BF16)
    lv = dma(p, "sync", xr[:, :, :],
             ctx.T["xn"].rearrange("(g p) t -> p g t", p=128))
    p.wait("tensor", "ld", lv)
    rbf = Ring(ctx, "rvk_eb", (128, NT), BF16, n=3)
    rf1 = Ring(ctx, "rvk_ef", (128, NT), F32, n=3)
    rf2 = Ring(ctx, "rvk_e2", (128, NT), F32, n=3)
    rvr = Ring(ctx, "rvk_vr", (128, NT), F32, n=3)

    def evac(oc, o, n, b):
        gi = oc % G
        r = slice(gi * 128, (gi + 1) * 128)
        p.wait("scalar", "mm")
        if oc < G:
            buf, s = rbf.take("scalar")
            ev = op(p, "scalar", "activation", buf[:, :n], ctx.ps[b][:, :n],
                    AF.Sigmoid, sem="act")
            p.wait("sync", "act", ev)
            sv = dma(p, "sync", ctx.T["sr"][r, o:o + n], buf[:, :n], sem="st")
            rbf.done(s, "st", sv)
            ctx.bank_done(b, "act", ev)
        elif oc < 2 * G:
            buf, s = rf1.take("scalar")
            ev = op(p, "scalar", "copy", buf[:, :n], ctx.ps[b][:, :n], sem="act")
            p.wait("sync", "act", ev)
            sv = dma(p, "sync", ctx.T["vsc"][r, o:o + n], buf[:, :n], sem="st")
            rf1.done(s, "st", sv)
            ctx.bank_done(b, "act", ev)
        else:
            buf, s = rf1.take("scalar")
            ev = op(p, "scalar", "activation", buf[:, :n], ctx.ps[b][:, :n],
                    AF.Exp, sem="act")
            ctx.bank_done(b, "act", ev)
            vbuf, vs = rvr.take("sync")
            lv2 = dma(p, "sync", vbuf[:, :n], ctx.T["vsc"][r, o:o + n])
            p.wait("sync", "act", ev)
            sv = dma(p, "sync", ctx.T["ek"][r, o:o + n], buf[:, :n], sem="st")
            p.wait("vector", "act", ev)
            p.wait("vector", "ld", lv2)
            obuf, os_ = rf2.take("vector")
            dv = op(p, "vector", "tensor_mul", obuf[:, :n], buf[:, :n],
                    vbuf[:, :n], sem="dve")
            rf1.done(s, "dve", dv)
            rvr.done(vs, "dve", dv)
            p.wait("sync", "dve", dv)
            s2 = dma(p, "sync", ctx.T["ekv"][r, o:o + n], obuf[:, :n], sem="st")
            rf2.done(os_, "st", s2)

    proj_stream(ctx, "wrvkT", lambda kc, o, n: xr[:, kc, o:o + n], TW, evac,
                ocs=3 * G)


@phase("wkv", ["ek", "ekv", "sr", "td_fm", "tf_fm", "wmask"], ["y"])
def ph_wkv(ctx):
    p = ctx.p
    tdc = ctx.alloc("wk_td", (128, G), F32)
    tfc = ctx.alloc("wk_tf", (128, G), F32)
    wmc = ctx.alloc("wk_wm", (128, 1), F32)
    alph = ctx.alloc("wk_al", (128, G), F32)
    euc = ctx.alloc("wk_eu", (128, G), F32)
    dma(p, "sync", tdc[:, :], ctx.T["td_fm"])
    dma(p, "sync", tfc[:, :], ctx.T["tf_fm"])
    lv = dma(p, "sync", wmc[:, :], ctx.T["wmask"])
    p.wait("scalar", "ld", lv)
    op(p, "scalar", "activation", alph[:, :], tdc[:, :], AF.Exp)
    op(p, "scalar", "activation", alph[:, :], alph[:, :], AF.Exp, scale=-1.0)
    op(p, "scalar", "activation", euc[:, :], tfc[:, :], AF.Exp)
    op(p, "scalar", "copy", alph[:, :], alph[:, :])
    av = op(p, "scalar", "copy", euc[:, :], euc[:, :], sem="act")
    ekr = Ring(ctx, "wk_ek", (128, TW), F32, n=2)
    ekvr = Ring(ctx, "wk_ekv", (128, TW), F32, n=2)
    srr = Ring(ctx, "wk_sr", (128, TH), BF16, n=2)
    yr_ = Ring(ctx, "wk_y", (128, TH), BF16, n=2)
    sS = ctx.alloc("wk_S", (128, TW), F32)
    sZ = ctx.alloc("wk_Z", (128, TW), F32)
    al = ctx.alloc("wk_ab", (128, TW), F32)
    onesw = ctx.alloc("wk_1", (128, TW), F32)
    num = ctx.alloc("wk_nu", (128, TH), F32)
    den = ctx.alloc("wk_de", (128, TH), F32)
    p.wait("vector", "act", av)
    op(p, "vector", "memset", onesw[:, :], 1.0)
    for g in range(G):
        r = slice(g * 128, (g + 1) * 128)
        ekg, eks = ekr.take("sync")
        ekvg, ekvs = ekvr.take("sync")
        srg, srs = srr.take("sync")
        dma(p, "sync", ekg[:, :], ctx.T["ek"][r, :])
        dma(p, "sync", ekvg[:, :], ctx.T["ekv"][r, :])
        l3 = dma(p, "sync", srg[:, :], ctx.T["sr"][r, J0:TW])
        p.wait("vector", "ld", l3)
        op(p, "vector", "tensor_scalar_mul", al[:, :], onesw[:, :],
           alph[:, g:g + 1])
        op(p, "vector", "tensor_scalar_mul", ekg[:, 0:WARM], ekg[:, 0:WARM],
           wmc[:, 0:1])
        op(p, "vector", "tensor_scalar_mul", ekvg[:, 0:WARM], ekvg[:, 0:WARM],
           wmc[:, 0:1])
        op(p, "vector", "tensor_tensor_scan", sS[:, :], al[:, :], ekvg[:, :],
           0.0, ALU.mult, ALU.add)
        dv = op(p, "vector", "tensor_tensor_scan", sZ[:, :], al[:, :],
                ekg[:, :], 0.0, ALU.mult, ALU.add, sem="dve")
        p.wait("vector", "dve", dv)
        op(p, "vector", "scalar_tensor_tensor", num[:, :], ekvg[:, J0:TW],
           euc[:, g:g + 1], sS[:, J0 - 1:TW - 1], ALU.mult, ALU.add)
        dv = op(p, "vector", "scalar_tensor_tensor", den[:, :], ekg[:, J0:TW],
                euc[:, g:g + 1], sZ[:, J0 - 1:TW - 1], ALU.mult, ALU.add,
                sem="dve")
        ekr.done(eks, "dve", dv)
        ekvr.done(ekvs, "dve", dv)
        dv = op(p, "vector", "tensor_scalar_max", den[:, :], den[:, :], 1e-30,
                sem="dve")
        p.wait("scalar", "dve", dv)
        op(p, "scalar", "activation", den[:, :], den[:, :], AF.Ln)
        av = op(p, "scalar", "activation", den[:, :], den[:, :], AF.Exp,
                scale=-1.0, sem="act")
        p.wait("vector", "act", av)
        op(p, "vector", "tensor_mul", num[:, :], num[:, :], den[:, :])
        ybf, ys = yr_.take("vector")
        dv = op(p, "vector", "tensor_mul", ybf[:, :], num[:, :], srg[:, :],
                sem="dve")
        srr.done(srs, "dve", dv)
        p.wait("sync", "dve", dv)
        sv = dma(p, "sync", ctx.T["y"][r, :], ybf[:, :], sem="st")
        yr_.done(ys, "st", sv)


def make_copy_evac(ctx, dst_name, ring):
    p = ctx.p

    def evac(oc, o, n, b):
        r = slice(oc * 128, (oc + 1) * 128)
        p.wait("scalar", "mm")
        buf, s = ring.take("scalar")
        ev = op(p, "scalar", "copy", buf[:, :n], ctx.ps[b][:, :n], sem="act")
        ctx.bank_done(b, "act", ev)
        p.wait("sync", "act", ev)
        sv = dma(p, "sync", ctx.T[dst_name][r, o:o + n], buf[:, :n], sem="st")
        ring.done(s, "st", sv)
    return evac


@phase("wout", ["y", "woutT"], ["attn0"])
def ph_wout(ctx):
    p = ctx.p
    yr = ctx.alloc("wo_y", (128, G, TH), BF16)
    lv = dma(p, "sync", yr[:, :, :],
             ctx.T["y"].rearrange("(g p) t -> p g t", p=128))
    p.wait("tensor", "ld", lv)
    ring = Ring(ctx, "wo_ev", (128, NT), F32, n=3)
    proj_stream(ctx, "woutT", lambda kc, o, n: yr[:, kc, o:o + n], TH,
                make_copy_evac(ctx, "attn0", ring))


@phase("router", ["xn", "gatewT"], ["tw"])
def ph_router(ctx):
    p = ctx.p
    gw = ctx.alloc("rt_gw", (128, G, E), BF16)
    lv = dma(p, "gpsimd", gw[:, :, :],
             ctx.T["gatewT"].rearrange("(g p) e -> p g e", p=128))
    xtr = ctx.alloc("rt_xnres", (128, G, TH), BF16)
    lxr = dma(p, "sync", xtr[:, :, :],
              ctx.T["xn"][:, J0:TW].rearrange("(g p) t -> p g t", p=128))
    lg = ctx.alloc("rt_lg", (128, E), F32)
    twt = ctx.alloc("rt_tw", (128, E), F32)
    rex = ctx.alloc("rt_ex", (128, E), F32)
    r1 = ctx.alloc("rt_r1", (128, 2), F32)
    twT = ctx.alloc("rt_twT", (E, TH), F32)
    p.wait("tensor", "gg", lv)
    ntile = (TH + 127) // 128
    for ti in range(ntile):
        o = J0 + ti * 128
        n = min(128, TW - o)
        b = ctx.bank()
        p.wait("tensor", "ld", lxr)
        oo_ = ti * 128
        for g in range(G):
            op(p, "tensor", "matmul", ctx.ps[b][0:n, 0:E],
               xtr[:, g, oo_:oo_ + n], gw[:, g, :], start=(g == 0),
               stop=(g == G - 1), sem="mm" if g == G - 1 else None)
        p.wait("vector", "mm")
        p.wait("vector", "act")

        def vstep(meth, *a, **kw):
            _d = op(p, "vector", meth, *a, sem="dve", **kw)
            p.wait("vector", "dve", _d)
            return _d

        vstep("tensor_copy", lg[:n, :], ctx.ps[b][0:n, 0:E])
        ctx.bank_done(b, "dve", p.cnt["dve"])
        vstep("reduce_max", r1[:n, 0:1], lg[:n, :], AX.X)
        vstep("tensor_scalar_sub", lg[:n, :], lg[:n, :], r1[:n, 0:1])
        vstep("tensor_single_scalar", twt[:n, :], lg[:n, :], 0.0, ALU.is_ge)
        vstep("tensor_scalar_mul", twt[:n, :], twt[:n, :], -1e30)
        vstep("tensor_add", twt[:n, :], twt[:n, :], lg[:n, :])
        dv = vstep("reduce_max", r1[:n, 1:2], twt[:n, :], AX.X)
        p.wait("scalar", "dve", dv)
        av = op(p, "scalar", "activation", rex[:n, :], lg[:n, :], AF.Exp,
                sem="act")
        p.wait("vector", "act", av)
        vstep("tensor_single_scalar", twt[:n, :], lg[:n, :], r1[:n, 1:2],
              ALU.is_ge)
        vstep("tensor_mul", twt[:n, :], twt[:n, :], rex[:n, :])
        dv = vstep("tensor_reduce", r1[:n, 0:1], twt[:n, :], AX.X, ALU.add)
        p.wait("scalar", "dve", dv)
        p.wait("scalar", "act")
        op(p, "scalar", "activation", r1[:n, 0:1], r1[:n, 0:1], AF.Ln)
        op(p, "scalar", "activation", r1[:n, 0:1], r1[:n, 0:1], AF.Exp,
           scale=-1.0)
        av = op(p, "scalar", "copy", r1[:n, 0:1], r1[:n, 0:1], sem="act")
        p.wait("vector", "act", av)
        dv = vstep("tensor_scalar_mul", twt[:n, :], twt[:n, :], r1[:n, 0:1])
        p.wait("tensor", "dve", dv)
        b2 = ctx.bank()
        op(p, "tensor", "transpose", ctx.ps[b2][0:E, 0:128], twt[:, 0:E],
           ctx.gl["ident"][:, :], sem="mm")
        p.wait("vector", "mm")
        dv = op(p, "vector", "tensor_copy", twT[0:E, ti * 128:ti * 128 + n],
                ctx.ps[b2][0:E, 0:n], sem="dve")
        ctx.bank_done(b2, "dve", dv)
    row1 = ctx.alloc("rt_row", (1, TH), F32)
    ring = Ring(ctx, "rt_ev", (128, NT), BF16, n=3)
    for e in range(E):
        p.wait_all("sync", ("dve", "ld", "mm"))
        l3 = dma(p, "sync", row1[0:1, :], twT[e:e + 1, 0:TH])
        p.wait("tensor", "ld", l3)
        for (o, n) in tiles(TH):
            b = ctx.bank()
            op(p, "tensor", "matmul", ctx.ps[b][:, :n], ctx.gl["ones_r"][0:1, :],
               row1[0:1, o:o + n], start=True, stop=True, sem="mm")
            p.wait("scalar", "mm")
            buf, s = ring.take("scalar")
            ev = op(p, "scalar", "copy", buf[:, :n], ctx.ps[b][:, :n], sem="act")
            ctx.bank_done(b, "act", ev)
            p.wait("sync", "act", ev)
            sv = dma(p, "sync", ctx.T["tw"][e * 128:(e + 1) * 128, o:o + n],
                     buf[:, :n], sem="st")
            ring.done(s, "st", sv)


@phase("moe", ["xn", "tw", "mgateT", "mupT", "mdownT"], ["ffn0"])
def ph_moe(ctx):
    p = ctx.p
    HL = 1027
    acc = ctx.alloc("mo_acc", (128, G, HL), F32)
    hall = ctx.alloc("mo_h", (128, FC_M, HL), BF16)
    twb = ctx.alloc("mo_tw", (128, HL), BF16)
    xnt = ctx.alloc("mo_xn", (128, G, HL), BF16)
    gring = Ring(ctx, "mo_wg", (128, G, 128), BF16, n=2)
    uring = Ring(ctx, "mo_wu", (128, G, 128), BF16, n=2)
    dring = Ring(ctx, "mo_wd", (128, FC_M, 128), BF16, n=2)
    sb1 = ctx.alloc("mo_e1", (128, NT), BF16)
    sb2 = ctx.alloc("mo_e2", (128, NT), BF16)
    for (h0, hn) in ((0, 1024), (1024, TH - 1024)):
        p.wait_all("vector", ("dve", "mm", "st"))
        op(p, "vector", "memset", acc[:, :, 0:hn], 0.0, sem="dve")
        htl = tiles(hn)
        p.wait_all("sync", ("mm",))
        for g in range(G):
            dma(p, "sync", xnt[:, g, 0:hn],
                ctx.T["xn"][g * 128:(g + 1) * 128, J0 + h0:J0 + h0 + hn])
        lvx_all = p.cnt["ld"]
        for e in range(E):
            p.wait_all("gpsimd", ("dve",))
            lvtw = dma(p, "gpsimd", twb[:, 0:hn],
                       ctx.T["tw"][e * 128:(e + 1) * 128, h0:h0 + hn])
            lastdv = None
            for fc in range(FC_M):
                gb, gs, v1 = stream_wcols(
                    ctx, p, ctx.T["mgateT"][e, :, fc * 128:(fc + 1) * 128]
                    if False else ctx.T["mgateT"][e], fc, G, gring)
                ub, us, v2 = stream_wcols(ctx, p, ctx.T["mupT"][e], fc, G,
                                          uring)
                p.wait("tensor", "gg", v2)
                lastg = lastu = None
                for (o, n) in htl:
                    p.wait("tensor", "ld", lvx_all)
                    bg = ctx.bank()
                    bu = ctx.bank()
                    for g in range(G):
                        lastg = op(p, "tensor", "matmul", ctx.ps[bg][:, :n],
                                   gb[:, g, :], xnt[:, g, o:o + n],
                                   start=(g == 0), stop=(g == G - 1),
                                   sem="mm" if g == G - 1 else None)
                    for g in range(G):
                        lastu = op(p, "tensor", "matmul", ctx.ps[bu][:, :n],
                                   ub[:, g, :], xnt[:, g, o:o + n],
                                   start=(g == 0), stop=(g == G - 1),
                                   sem="mm" if g == G - 1 else None)
                    p.wait("scalar", "mm")
                    p.wait("scalar", "dve")
                    op(p, "scalar", "activation", sb1[:, :n], ctx.ps[bg][:, :n],
                       AF.Silu)
                    av = op(p, "scalar", "copy", sb2[:, :n], ctx.ps[bu][:, :n],
                            sem="act")
                    ctx.bank_done(bg, "act", av)
                    ctx.bank_done(bu, "act", av)
                    p.wait("vector", "act", av)
                    p.wait("vector", "gg", lvtw)
                    op(p, "vector", "tensor_mul", sb1[:, :n], sb1[:, :n],
                       sb2[:, :n])
                    lastdv = op(p, "vector", "tensor_mul",
                                hall[:, fc, o:o + n], sb1[:, :n],
                                twb[:, o:o + n], sem="dve")
                gring.done(gs, "mm", lastg)
                uring.done(us, "mm", lastu)
            for oc in range(G):
                db, ds, vd = stream_wcols(ctx, p, ctx.T["mdownT"][e], oc,
                                          FC_M, dring)
                p.wait("tensor", "gg", vd)
                p.wait("tensor", "dve", lastdv)
                lastd = None
                for (o, n) in htl:
                    b = ctx.bank()
                    for fc in range(FC_M):
                        lastd = op(p, "tensor", "matmul", ctx.ps[b][:, :n],
                                   db[:, fc, :], hall[:, fc, o:o + n],
                                   start=(fc == 0), stop=(fc == FC_M - 1),
                                   sem="mm" if fc == FC_M - 1 else None)
                    p.wait("vector", "mm")
                    dv = op(p, "vector", "tensor_add", acc[:, oc, o:o + n],
                            acc[:, oc, o:o + n], ctx.ps[b][:, :n], sem="dve")
                    ctx.bank_done(b, "dve", dv)
                dring.done(ds, "mm", lastd)
        dv = p.cnt["dve"]
        p.wait("sync", "dve", dv)
        for oc in range(G):
            dma(p, "sync", ctx.T["ffn0"][oc * 128:(oc + 1) * 128, h0:h0 + hn],
                acc[:, oc, 0:hn], sem="st")
        p.wait("vector", "st")


@phase("x1ln", ["x_fm", "attn0", "ffn0"], ["x1", "xn1"])
def ph_x1ln(ctx):
    p = ctx.p
    xg = ctx.alloc("ln_xg", (128, NT), F32)
    t1 = ctx.alloc("x1_t1", (128, NT), F32)
    t2 = ctx.alloc("x1_t2", (128, NT), F32)
    for g in range(G):
        r = slice(g * 128, (g + 1) * 128)
        for (o, n) in tiles(TH):
            p.wait_all("sync", ("dve",))
            p.wait_all("scalar", ("dve",))
            p.wait_all("gpsimd", ("dve",))
            dma(p, "sync", xg[:, :n], ctx.T["x_fm"][r, J0 + o:J0 + o + n])
            dma(p, "scalar", t1[:, :n], ctx.T["attn0"][r, o:o + n], sem="ld")
            lv = dma(p, "gpsimd", t2[:, :n], ctx.T["ffn0"][r, o:o + n],
                     sem="ld")
            p.wait("vector", "ld", lv)
            op(p, "vector", "tensor_add", xg[:, :n], xg[:, :n], t1[:, :n])
            dv = op(p, "vector", "tensor_add", xg[:, :n], xg[:, :n], t2[:, :n],
                    sem="dve")
            p.wait("sync", "dve", dv)
            dma(p, "sync", ctx.T["x1"][r, o:o + n], xg[:, :n], sem="st")
    p.barrier()
    full_ln(ctx, "x1", TH, "xn1")


@phase("conv", ["xn1", "convw_fm", "convb_fm"], ["xc"])
def ph_conv(ctx):
    p = ctx.p
    cw = ctx.alloc("cv_w", (128, G * 4), F32)
    cb = ctx.alloc("cv_b", (128, G), F32)
    xg = ctx.alloc("cv_x", (128, TH), BF16)
    accv = ctx.alloc("cv_a", (128, TM), F32)
    ob = ctx.alloc("cv_o", (128, TM), BF16)
    dma(p, "sync", cw[:, :], ctx.T["convw_fm"])
    lv = dma(p, "sync", cb[:, :], ctx.T["convb_fm"])
    for g in range(G):
        r = slice(g * 128, (g + 1) * 128)
        p.wait_all("sync", ("dve", "act"))
        l2 = dma(p, "sync", xg[:, :], ctx.T["xn1"][r, :])
        p.wait("vector", "ld", max(lv, l2))
        p.wait("vector", "act")
        op(p, "vector", "tensor_scalar_mul", accv[:, :], xg[:, 0:TM],
           cw[:, g * 4:g * 4 + 1])
        for tap in range(1, 4):
            last = op(p, "vector", "scalar_tensor_tensor", accv[:, :],
                      xg[:, tap:tap + TM], cw[:, g * 4 + tap:g * 4 + tap + 1],
                      accv[:, :], ALU.mult, ALU.add,
                      sem="dve" if tap == 3 else None)
        p.wait("scalar", "dve", last)
        p.wait("scalar", "st")
        av = op(p, "scalar", "activation", ob[:, :], accv[:, :], AF.Silu,
                bias=cb[:, g:g + 1], sem="act")
        p.wait("sync", "act", av)
        dma(p, "sync", ctx.T["xc"][r, :], ob[:, :], sem="st")


@phase("l1mm", ["xn1", "xc", "wqT", "wkT", "wvT", "igwT", "ogwT", "gammawT",
                "igb_fm", "ogb_fm", "gammab_bc"],
       ["qh", "kh", "vh", "ig", "og", "afull"])
def ph_l1mm(ctx):
    p = ctx.p
    xr = ctx.alloc("l1_xn", (128, G, TM), BF16)
    xcr = ctx.alloc("l1_xc", (128, G, TM), BF16)
    lv = dma(p, "sync", xr[:, :, :],
             ctx.T["xn1"][:, HALO:TH].rearrange("(g p) t -> p g t", p=128))
    lv = dma(p, "sync", xcr[:, :, :],
             ctx.T["xc"].rearrange("(g p) t -> p g t", p=128))
    igb = ctx.alloc("l1_igb", (128, G), F32)
    ogb = ctx.alloc("l1_ogb", (128, G), F32)
    gmb = ctx.alloc("l1_gmb", (128, H), F32)
    dma(p, "sync", igb[:, :], ctx.T["igb_fm"])
    dma(p, "sync", ogb[:, :], ctx.T["ogb_fm"])
    lv = dma(p, "sync", gmb[:, :], ctx.T["gammab_bc"])
    p.wait("tensor", "ld", lv)
    p.wait("scalar", "ld", lv)
    rf = Ring(ctx, "l1_ef", (128, NT), F32, n=3)
    rb = Ring(ctx, "l1_eb", (128, NT), BF16, n=3)
    for wname, dst in (("wqT", "qh"), ("wkT", "kh"), ("wvT", "vh")):
        proj_stream(ctx, wname, lambda kc, o, n: xr[:, kc, o:o + n], TM,
                    make_copy_evac(ctx, dst, rf))

    def gate_evac(dst, bias):
        def f(oc, o, n, b):
            r = slice(oc * 128, (oc + 1) * 128)
            p.wait("scalar", "mm")
            buf, s = rb.take("scalar")
            ev = op(p, "scalar", "activation", buf[:, :n], ctx.ps[b][:, :n],
                    AF.Sigmoid, bias=bias[:, oc:oc + 1], sem="act")
            ctx.bank_done(b, "act", ev)
            p.wait("sync", "act", ev)
            sv = dma(p, "sync", ctx.T[dst][r, o:o + n], buf[:, :n], sem="st")
            rb.done(s, "st", sv)
        return f

    proj_stream(ctx, "igwT", lambda kc, o, n: xcr[:, kc, o:o + n], TM,
                gate_evac("ig", igb))
    proj_stream(ctx, "ogwT", lambda kc, o, n: xcr[:, kc, o:o + n], TM,
                gate_evac("og", ogb))
    gw = ctx.alloc("l1_gw", (128, G, H), BF16)
    lv = dma(p, "gpsimd", gw[:, :, :],
             ctx.T["gammawT"].rearrange("(g p) h -> p g h", p=128))
    p.wait("tensor", "gg", lv)
    aT = ctx.alloc("l1_aT", (H, TM), F32)
    alog = ctx.alloc("l1_alog", (128, H), F32)
    for ti in range(TM // 128):
        o = ti * 128
        b = ctx.bank()
        for g in range(G):
            op(p, "tensor", "matmul", ctx.ps[b][0:128, 0:H], xcr[:, g, o:o + 128],
               gw[:, g, :], start=(g == 0), stop=(g == G - 1),
               sem="mm" if g == G - 1 else None)
        p.wait("vector", "mm")
        p.wait("vector", "act")
        dv = op(p, "vector", "tensor_add", alog[:, :], ctx.ps[b][0:128, 0:H],
                gmb[:, :], sem="dve")
        ctx.bank_done(b, "dve", dv)
        p.wait("scalar", "dve", dv)
        op(p, "scalar", "activation", alog[:, :], alog[:, :], AF.Sigmoid)
        av = op(p, "scalar", "copy", alog[:, :], alog[:, :], sem="act")
        p.wait("tensor", "act", av)
        b2 = ctx.bank()
        op(p, "tensor", "transpose", ctx.ps[b2][0:H, 0:128], alog[:, 0:H],
           ctx.gl["ident"][:, :], sem="mm")
        p.wait("vector", "mm")
        dv = op(p, "vector", "tensor_copy", aT[0:H, o:o + 128],
                ctx.ps[b2][0:H, 0:128], sem="dve")
        ctx.bank_done(b2, "dve", dv)
    row1 = ctx.alloc("l1_row", (1, TM), F32)
    for h in range(H):
        p.wait_all("sync", ("dve", "ld", "mm"))
        l3 = dma(p, "sync", row1[0:1, :], aT[h:h + 1, :])
        p.wait("tensor", "ld", l3)
        for (o, n) in tiles(TM):
            b = ctx.bank()
            op(p, "tensor", "matmul", ctx.ps[b][:, :n], ctx.gl["ones_r"][0:1, :],
               row1[0:1, o:o + n], start=True, stop=True, sem="mm")
            p.wait("scalar", "mm")
            buf, s = rf.take("scalar")
            ev = op(p, "scalar", "copy", buf[:, :n], ctx.ps[b][:, :n], sem="act")
            ctx.bank_done(b, "act", ev)
            p.wait("sync", "act", ev)
            sv = dma(p, "sync", ctx.T["afull"][h * 128:(h + 1) * 128, o:o + n],
                     buf[:, :n], sem="st")
            rf.done(s, "st", sv)


@phase("norms", ["qh", "kh", "vh", "ig", "vng", "vnb"],
       ["qn", "kn", "vn", "bfull"])
def ph_norms(ctx):
    p = ctx.p
    vng = ctx.alloc("nm_vg", (128, 1), F32)
    vnb = ctx.alloc("nm_vb", (128, 1), F32)
    dma(p, "sync", vng[:, :], ctx.T["vng"])
    lv = dma(p, "sync", vnb[:, :], ctx.T["vnb"])
    xg = ctx.alloc("nm_x", (128, TM), F32)
    t1 = ctx.alloc("nm_t1", (128, TM), F32)
    t2 = ctx.alloc("nm_t2", (128, TM), F32)
    kg = ctx.alloc("nm_k", (128, TM), F32)
    igt = ctx.alloc("nm_ig", (128, TM), BF16)
    rows = ctx.alloc("nm_rows", (1, 3 * TM), F32)
    for g in range(G):
        r = slice(g * 128, (g + 1) * 128)
        for name, dstn in (("qh", "qn"), ("kh", "kn")):
            p.wait_all("sync", ("dve", "act", "st"))
            l2 = dma(p, "sync", xg[:, :], ctx.T[name][r, :])
            p.wait("scalar", "ld", l2)
            rq, _ = head_stats(ctx, xg, TM, rows, False, l2=True)
            bcast_row(ctx, rq, t1, TM)
            p.wait("vector", "act")
            dv = op(p, "vector", "tensor_mul", xg[:, :], xg[:, :], t1[:, :],
                    sem="dve")
            p.wait("sync", "dve", dv)
            dma(p, "sync", ctx.T[dstn][r, :], xg[:, :], sem="st")
            if name == "kh":
                op(p, "vector", "tensor_copy", kg[:, :], xg[:, :], sem="dve")
        p.wait_all("sync", ("dve", "act", "st"))
        l2 = dma(p, "sync", xg[:, :], ctx.T["vh"][r, :])
        p.wait("scalar", "ld", l2)
        rq, rm = head_stats(ctx, xg, TM, rows, True)
        bcast_row(ctx, rq, t1, TM)
        bcast_row(ctx, rm, t2, TM)
        l4 = dma(p, "sync", igt[:, :], ctx.T["ig"][r, :])
        p.wait("vector", "act")
        p.wait("vector", "ld", l4)
        op(p, "vector", "tensor_sub", xg[:, :], xg[:, :], t2[:, :])
        op(p, "vector", "tensor_mul", xg[:, :], xg[:, :], t1[:, :])
        op(p, "vector", "tensor_scalar", xg[:, :], xg[:, :], vng[:, 0:1],
           vnb[:, 0:1], ALU.mult, ALU.add, sem="dve")
        dv = p.cnt["dve"]
        p.wait("sync", "dve", dv)
        dma(p, "sync", ctx.T["vn"][r, :], xg[:, :], sem="st")
        p.wait("vector", "st")
        op(p, "vector", "tensor_mul", xg[:, :], xg[:, :], kg[:, :])
        dv = op(p, "vector", "tensor_mul", xg[:, :], xg[:, :], igt[:, :],
                sem="dve")
        p.wait("sync", "dve", dv)
        dma(p, "sync", ctx.T["bfull"][r, :], xg[:, :], sem="st")


@phase("scan", ["afull", "bfull", "selmask"], ["mem"])
def ph_scan(ctx):
    p = ctx.p
    ag = ctx.alloc("sc_a", (128, TM), F32)
    bg = ctx.alloc("sc_b", (128, TM), F32)
    Bl = ctx.alloc("sc_Bl", (128, TM), F32)
    Btot = ctx.alloc("sc_Bt", (128, G), F32)
    selc = ctx.alloc("sc_sel", (128, 8), F32)
    gath = ctx.alloc("sc_g", (128, 8, G), F32)
    carry = ctx.alloc("sc_c", (128, G), F32)
    lv = dma(p, "sync", selc[:, :], ctx.T["selmask"])
    for g in range(G):
        r = slice(g * 128, (g + 1) * 128)
        p.wait_all("sync", ("dve",))
        dma(p, "sync", ag[:, :], ctx.T["afull"][r, :])
        l2 = dma(p, "sync", bg[:, :], ctx.T["bfull"][r, :])
        p.wait("vector", "ld", l2)
        dv = op(p, "vector", "tensor_tensor_scan", Bl[:, :], ag[:, :],
                bg[:, :], 0.0, ALU.mult, ALU.add, sem="dve")
        p.wait("vector", "dve", dv)
        op(p, "vector", "tensor_copy", Btot[:, g:g + 1], Bl[:, TM - 1:TM],
           sem="dve")
    dv = p.cnt["dve"]
    p.wait("sync", "dve", dv)
    sv = dma(p, "sync", ctx.T["cc_in"].rearrange("o (p g) -> (o p) g", p=128),
             Btot[:, :], sem="st")
    p.wait("gpsimd", "st", sv)
    ccv = p.bump("cc", 1)

    def fcc(e):
        e.collective_compute(
            "AllGather", ALU.bypass, replica_groups=[list(range(8))],
            ins=[ctx.T["cc_in"].opt()], outs=[ctx.T["cc_out"].opt()],
        ).then_inc(_S["cc"], 1)

    p.add("gpsimd", fcc)
    p.wait("gpsimd", "cc", ccv)
    gl = dma(p, "gpsimd", gath[:, :, :],
             ctx.T["cc_out"].rearrange("r (p g) -> p r g", p=128))
    p.wait("vector", "gg", gl)
    p.wait("vector", "ld", lv)
    op(p, "vector", "memset", carry[:, :], 0.0)
    for rk in range(8):
        dv = op(p, "vector", "scalar_tensor_tensor", carry[:, :],
                gath[:, rk, :], selc[:, rk:rk + 1], carry[:, :], ALU.mult,
                ALU.add, sem="dve")
        p.wait("vector", "dve", dv)
    for g in range(G):
        r = slice(g * 128, (g + 1) * 128)
        p.wait_all("sync", ("dve",))
        dma(p, "sync", ag[:, :], ctx.T["afull"][r, :])
        l2 = dma(p, "sync", bg[:, :], ctx.T["bfull"][r, :])
        p.wait("vector", "ld", l2)
        p.wait("vector", "st")
        dv = op(p, "vector", "tensor_tensor_scan", Bl[:, :], ag[:, :], bg[:, :],
                carry[:, g:g + 1], ALU.mult, ALU.add, sem="dve")
        p.wait("vector", "dve", dv)
        p.wait("sync", "dve", dv)
        dma(p, "sync", ctx.T["mem"][r, :], Bl[:, :], sem="st")


@phase("mixout", ["mem", "qn", "og", "mng", "mnb", "gng_fm", "gnb_fm"], ["oo"])
def ph_mixout(ctx):
    p = ctx.p
    mng = ctx.alloc("mx_mg", (128, 1), F32)
    mnb = ctx.alloc("mx_mb", (128, 1), F32)
    gng = ctx.alloc("mx_gg", (128, G), F32)
    gnb = ctx.alloc("mx_gb", (128, G), F32)
    dma(p, "sync", mng[:, :], ctx.T["mng"])
    dma(p, "sync", mnb[:, :], ctx.T["mnb"])
    dma(p, "sync", gng[:, :], ctx.T["gng_fm"])
    lv = dma(p, "sync", gnb[:, :], ctx.T["gnb_fm"])
    xg = ctx.alloc("mx_x", (128, TM), F32)
    t1 = ctx.alloc("mx_t1", (128, TM), F32)
    t2 = ctx.alloc("mx_t2", (128, TM), F32)
    qt = ctx.alloc("mx_q", (128, TM), F32)
    ogt = ctx.alloc("mx_og", (128, TM), BF16)
    ob = ctx.alloc("mx_o", (128, TM), BF16)
    rows = ctx.alloc("mx_rows", (1, 3 * TM), F32)
    for g in range(G):
        r = slice(g * 128, (g + 1) * 128)
        p.wait_all("sync", ("dve", "act", "st"))
        p.wait("vector", "st")
        l2 = dma(p, "sync", xg[:, :], ctx.T["mem"][r, :])
        p.wait("scalar", "ld", l2)
        rq, rm = head_stats(ctx, xg, TM, rows, True)
        bcast_row(ctx, rq, t1, TM)
        bcast_row(ctx, rm, t2, TM)
        l3 = dma(p, "sync", qt[:, :], ctx.T["qn"][r, :])
        p.wait("vector", "act")
        p.wait("vector", "ld", l3)
        op(p, "vector", "tensor_sub", xg[:, :], xg[:, :], t2[:, :])
        op(p, "vector", "tensor_mul", xg[:, :], xg[:, :], t1[:, :])
        op(p, "vector", "tensor_scalar", xg[:, :], xg[:, :], mng[:, 0:1],
           mnb[:, 0:1], ALU.mult, ALU.add)
        dv = op(p, "vector", "tensor_mul", xg[:, :], xg[:, :], qt[:, :],
                sem="dve")
        p.wait("scalar", "dve", dv)
        rq, rm = head_stats(ctx, xg, TM, rows, True)
        bcast_row(ctx, rq, t1, TM)
        bcast_row(ctx, rm, t2, TM)
        l4 = dma(p, "sync", ogt[:, :], ctx.T["og"][r, :])
        p.wait("vector", "act")
        p.wait("vector", "ld", l4)
        op(p, "vector", "tensor_sub", xg[:, :], xg[:, :], t2[:, :])
        op(p, "vector", "tensor_mul", xg[:, :], xg[:, :], t1[:, :])
        op(p, "vector", "tensor_scalar", xg[:, :], xg[:, :], gng[:, g:g + 1],
           gnb[:, g:g + 1], ALU.mult, ALU.add)
        dv = op(p, "vector", "tensor_mul", ob[:, :], xg[:, :], ogt[:, :],
                sem="dve")
        p.wait("sync", "dve", dv)
        dma(p, "sync", ctx.T["oo"][r, :], ob[:, :], sem="st")


@phase("wo", ["oo", "woT"], ["attn1"])
def ph_wo(ctx):
    p = ctx.p
    yr = ctx.alloc("wq_o", (128, G, TM), BF16)
    lv = dma(p, "sync", yr[:, :, :],
             ctx.T["oo"].rearrange("(g p) t -> p g t", p=128))
    p.wait("tensor", "ld", lv)
    ring = Ring(ctx, "wq_ev", (128, NT), F32, n=3)
    proj_stream(ctx, "woT", lambda kc, o, n: yr[:, kc, o:o + n], TM,
                make_copy_evac(ctx, "attn1", ring))


@phase("ffn", ["xn1", "qup", "qgate", "qdown", "qscales", "x1", "attn1"],
       ["out_fm"])
def ph_ffn(ctx):
    p = ctx.p
    rows = ctx.alloc("ff_rows", (1, 16), F32)
    lv = dma(p, "sync", rows[0:1, :], ctx.T["qscales"][0:1, :])
    p.wait("vector", "ld", lv)
    op(p, "vector", "tensor_mul", rows[0:1, 12:13], rows[0:1, 5:6],
       rows[0:1, 6:7], sem="dve")
    supc = ctx.alloc("ff_su", (128, 1), F32)
    sgdc = ctx.alloc("ff_sg", (128, 1), F32)
    bcast_row(ctx, rows[0:1, 4:5], supc, 1)
    bcast_row(ctx, rows[0:1, 12:13], sgdc, 1)
    p.wait("scalar", "act")
    xr = ctx.alloc("ff_xn", (128, G, TM), BF16)
    lv = dma(p, "sync", xr[:, :, :],
             ctx.T["xn1"][:, HALO:TH].rearrange("(g p) t -> p g t", p=128))
    p.wait("tensor", "ld", lv)
    hall = ctx.alloc("ff_h", (128, FC_F, NT), BF16)
    uring = Ring(ctx, "ff_wu", (128, G, 128), BF16, n=2)
    gring = Ring(ctx, "ff_wg", (128, G, 128), BF16, n=2)
    dring = Ring(ctx, "ff_wd", (128, FC_F, 128), BF16, n=2)
    sb1 = ctx.alloc("ff_e1", (128, NT), BF16)
    sb2 = ctx.alloc("ff_e2", (128, NT), BF16)
    fx1 = ctx.alloc("ff_x1", (128, NT), F32)
    fat = ctx.alloc("ff_at", (128, NT), F32)
    oring = Ring(ctx, "ff_o", (128, NT), F32, n=3)
    for (o, n) in tiles(TM):
        lastdv = None
        for fc in range(FC_F):
            ub, us, v1 = stream_wcols(ctx, p, ctx.T["qup"], fc, G, uring)
            gb, gs, v2 = stream_wcols(ctx, p, ctx.T["qgate"], fc, G, gring)
            p.wait("tensor", "gg", v2)
            bu = ctx.bank()
            bg = ctx.bank()
            lastu = lastg = None
            for g in range(G):
                lastu = op(p, "tensor", "matmul", ctx.ps[bu][:, :n],
                           ub[:, g, :], xr[:, g, o:o + n], start=(g == 0),
                           stop=(g == G - 1), sem="mm" if g == G - 1 else None)
            for g in range(G):
                lastg = op(p, "tensor", "matmul", ctx.ps[bg][:, :n],
                           gb[:, g, :], xr[:, g, o:o + n], start=(g == 0),
                           stop=(g == G - 1), sem="mm" if g == G - 1 else None)
            uring.done(us, "mm", lastu)
            gring.done(gs, "mm", lastg)
            p.wait("scalar", "mm")
            p.wait("scalar", "dve")
            op(p, "scalar", "activation", sb1[:, :n], ctx.ps[bu][:, :n],
               AF.Silu, scale=supc[:, 0:1])
            av = op(p, "scalar", "copy", sb2[:, :n], ctx.ps[bg][:, :n],
                    sem="act")
            ctx.bank_done(bu, "act", av)
            ctx.bank_done(bg, "act", av)
            p.wait("vector", "act", av)
            lastdv = op(p, "vector", "tensor_mul", hall[:, fc, :n], sb1[:, :n],
                        sb2[:, :n], sem="dve")
        for oc in range(G):
            db, ds, vd = stream_wcols(ctx, p, ctx.T["qdown"], oc, FC_F, dring)
            p.wait("tensor", "gg", vd)
            p.wait("tensor", "dve", lastdv)
            b = ctx.bank()
            lastd = None
            for fc in range(FC_F):
                lastd = op(p, "tensor", "matmul", ctx.ps[b][:, :n],
                           db[:, fc, :], hall[:, fc, :n], start=(fc == 0),
                           stop=(fc == FC_F - 1),
                           sem="mm" if fc == FC_F - 1 else None)
            dring.done(ds, "mm", lastd)
            rr = slice(oc * 128, (oc + 1) * 128)
            p.wait_all("sync", ("dve",))
            dma(p, "sync", fx1[:, :n], ctx.T["x1"][rr, HALO + o:HALO + o + n])
            l2 = dma(p, "sync", fat[:, :n], ctx.T["attn1"][rr, o:o + n])
            p.wait("scalar", "mm")
            obuf, osl = oring.take("scalar")
            ev = op(p, "scalar", "activation", obuf[:, :n], ctx.ps[b][:, :n],
                    AF.Copy, scale=sgdc[:, 0:1], sem="act")
            ctx.bank_done(b, "act", ev)
            p.wait("vector", "act", ev)
            p.wait("vector", "ld", l2)
            op(p, "vector", "tensor_add", obuf[:, :n], obuf[:, :n], fx1[:, :n])
            dv = op(p, "vector", "tensor_add", obuf[:, :n], obuf[:, :n],
                    fat[:, :n], sem="dve")
            p.wait("sync", "dve", dv)
            sv = dma(p, "sync", ctx.T["out_fm"][rr, o:o + n], obuf[:, :n],
                     sem="st")
            oring.done(osl, "st", sv)


ALL_PHASES = [n for (n, _, _, _) in PHASES]
LAUNCH_GROUPS = [ALL_PHASES]


def _needed_later(name, group, specs):
    if specs[name][2] == "out":
        return True
    idx = max(ALL_PHASES.index(n) for n in group)
    for (n, r, w, _) in PHASES:
        if ALL_PHASES.index(n) > idx and name in r:
            return True
    return False


def build_launch(group, dbg_outputs=()):
    specs = tensor_specs()
    ph = {n: (r, w, f) for (n, r, w, f) in PHASES}
    reads, writes = set(), set()
    for n in group:
        r, w, _ = ph[n]
        reads.update(r)
        writes.update(w)
    nc = bass.Bass("TRN2", target_bir_lowering=False)
    T, innames, outnames = {}, [], []
    for name in sorted(reads | writes):
        shape, dt, kind = specs[name]
        if kind == "cc":
            continue
        if name in writes and (_needed_later(name, group, specs)
                               or name in dbg_outputs):
            T[name] = nc.dram_tensor(name, shape, dt, kind="ExternalOutput").ap()
            outnames.append(name)
        elif name in reads and name not in writes:
            T[name] = nc.dram_tensor(name, shape, dt, kind="ExternalInput").ap()
            innames.append(name)
        else:
            T[name] = nc.dram_tensor(name, shape, dt).ap()
    if "scan" in group:
        for name in ("cc_in", "cc_out"):
            shape, dt, _ = specs[name]
            T[name] = nc.dram_tensor(name, shape, dt).ap()

    p = Prog()
    es = ExitStack()
    with es:
        psum = [es.enter_context(nc.psum_tensor(f"pb{i}", [128, NT], F32))
                for i in range(8)]
        for s in SEMS:
            _S[s] = es.enter_context(nc.semaphore(s))
        ctx = Ctx(nc, p, T, psum)
        gstack = es.enter_context(ExitStack())
        ctx.stack = gstack
        consts(ctx)
        p.barrier()
        for n in group:
            _, _, f = ph[n]
            with ExitStack() as phstack:
                ctx.stack = phstack
                ctx.cache = {}
                f(ctx)
                p.barrier()
        p.wait("sync", "st")
        p.wait("sync", "gg")
        p.wait("sync", "ld")
        with nc.Block() as blk:
            for eng in ENGINES:
                fns = p.ops[eng]
                if not fns:
                    continue

                def make(fns):
                    def _(e):
                        for f in fns:
                            f(e)
                    return _
                getattr(blk, eng)(make(fns))
    return nc, innames, outnames


def fm(vec):
    return np.ascontiguousarray(
        np.asarray(vec, np.float32).reshape(G, 128).T)


def prep_inputs(inp):
    x = np.asarray(inp["L0_attn_x"], np.float32)
    B = x.shape[0]
    xpad = np.concatenate([np.zeros((B, WARM, C), np.float32), x], axis=1)
    tr = lambda a: np.ascontiguousarray(np.asarray(a, np.float32).swapaxes(-1, -2))
    sh = {
        "wrvkT": tr(inp["L0_attn_W_rvk"]),
        "woutT": tr(inp["L0_attn_W_out"]),
        "td_fm": fm(inp["L0_attn_time_decay"]),
        "tf_fm": fm(inp["L0_attn_time_first"]),
        "gatewT": tr(inp["L0_ffn_gate_w"]),
        "mgateT": tr(inp["L0_ffn_gate_proj"]),
        "mupT": tr(inp["L0_ffn_up_proj"]),
        "mdownT": tr(inp["L0_ffn_down_proj"]),
        "wqT": tr(inp["L1_attn_Wq"]),
        "wkT": tr(inp["L1_attn_Wk"]),
        "wvT": tr(inp["L1_attn_Wv"]),
        "woT": tr(inp["L1_attn_Wo"]),
        "igwT": tr(inp["L1_attn_ig_w"]),
        "igb_fm": fm(inp["L1_attn_ig_b"]),
        "ogwT": tr(inp["L1_attn_og_w"]),
        "ogb_fm": fm(inp["L1_attn_og_b"]),
        "gammawT": tr(inp["L1_attn_gamma_w"]),
        "gammab_bc": np.ascontiguousarray(np.broadcast_to(
            np.asarray(inp["L1_attn_gamma_b"], np.float32)[None, :], (128, H))),
        "vng": np.asarray(inp["L1_attn_vn_g"], np.float32).reshape(128, 1),
        "vnb": np.asarray(inp["L1_attn_vn_b"], np.float32).reshape(128, 1),
        "mng": np.asarray(inp["L1_attn_mn_g"], np.float32).reshape(128, 1),
        "mnb": np.asarray(inp["L1_attn_mn_b"], np.float32).reshape(128, 1),
        "gng_fm": fm(inp["L1_attn_gn_g"]),
        "gnb_fm": fm(inp["L1_attn_gn_b"]),
        "fupT": tr(inp["L1_ffn_W_up"]),
        "fgateT": tr(inp["L1_ffn_W_gate"]),
        "fdownT": tr(inp["L1_ffn_W_down"]),
        "convb_fm": fm(inp["L1_attn_conv_b"]),
    }
    cw = np.asarray(inp["L1_attn_conv_w"], np.float32)[:, 0, :]
    sh["convw_fm"] = np.ascontiguousarray(
        cw.reshape(G, 128, 4).transpose(1, 0, 2).reshape(128, G * 4))
    percore = []
    for c in range(8):
        b, half = c // 2, c % 2
        t0 = half * TM
        d = {
            "x_fm": np.ascontiguousarray(xpad[b, t0:t0 + TW, :].T),
            "wmask": np.full((128, 1), 1.0 if half else 0.0, np.float32),
        }
        sel = np.zeros((128, 8), np.float32)
        if half == 1:
            sel[:, c - 1] = 1.0
        d["selmask"] = sel
        percore.append(d)
    return sh, percore


def kernel(**inputs):
    shared, percore = prep_inputs(inputs)
    carry = [dict(pc) for pc in percore]
    for group in LAUNCH_GROUPS:
        nc, innames, outnames = build_launch(group)
        in_maps = []
        for c in range(8):
            m = {}
            for n in innames:
                m[n] = carry[c][n] if n in carry[c] else shared[n]
            in_maps.append(m)
        res = run_bass_kernel_spmd(nc, in_maps, core_ids=list(range(8)))
        for c in range(8):
            for n in outnames:
                carry[c][n] = np.asarray(res.results[c][n])
    x = np.asarray(inputs["L0_attn_x"], np.float32)
    B, T, _ = x.shape
    outa = np.empty((B, T, C), np.float32)
    for c in range(8):
        b, half = c // 2, c % 2
        outa[b, half * TM:(half + 1) * TM, :] = carry[c]["out_fm"].T
    return outa
